# version 09
# baseline (speedup 1.0000x reference)
import numpy as np
from contextlib import ExitStack
import concourse.bass as bass
import concourse.mybir as mybir
from concourse.bass_utils import run_bass_kernel_spmd

F32 = mybir.dt.float32
F16 = mybir.dt.float16
I32 = mybir.dt.int32
AF = mybir.ActivationFunctionType
OP = mybir.AluOpType
AX = mybir.AxisListType

B, S, D = 2, 8192, 1024
CH = 32
K64 = 64
TOK = 2048
NT = 16
NST = 4
EPS = 1e-5
FLAGV = 7


def build_nc():
    import os
    DBG = int(os.environ.get("KMK_DBG", "0"))
    nc = bass.Bass("TRN2", target_bir_lowering=False)

    x_own = nc.dram_tensor("x_own", [TOK, D], F32, kind="ExternalInput").ap()
    x_lite = nc.dram_tensor("x_lite", [TOK, D], F32, kind="ExternalInput").ap()
    wcatT = nc.dram_tensor("wcatT", [D, K64], F32, kind="ExternalInput").ap()
    gb8 = nc.dram_tensor("gb8", [128, 8, 2], F32, kind="ExternalInput").ap()
    bcat2 = nc.dram_tensor("bcat2", [K64, 2], F32, kind="ExternalInput").ap()
    woutaug = nc.dram_tensor("woutaug", [33, D], F32, kind="ExternalInput").ap()
    wfT = nc.dram_tensor("wfT", [D, D], F32, kind="ExternalInput").ap()
    bf = nc.dram_tensor("bfv", [1, D], F32, kind="ExternalInput").ap()
    ldownT = nc.dram_tensor("ldownT", [100, 4], F32, kind="ExternalInput").ap()
    daux = nc.dram_tensor("daux", [100, D], F32, kind="ExternalInput").ap()
    lupT = nc.dram_tensor("lupT", [4, 50], F32, kind="ExternalInput").ap()
    upT = nc.dram_tensor("upT", [50, D], F32, kind="ExternalInput").ap()
    ident_in = nc.dram_tensor("ident", [128, 128], F32, kind="ExternalInput").ap()
    rolei = nc.dram_tensor("rolei", [1, 8], I32, kind="ExternalInput").ap()
    y = nc.dram_tensor("y", [TOK, D], F32, kind="ExternalOutput").ap()

    dbg = {}
    if DBG:
        for nm, shape, dt in [
            ("dbg_mva", [128, NT, 2], F32), ("dbg_pre", [K64, TOK], F16),
            ("dbg_a", [CH, TOK], F16), ("dbg_u", [CH, TOK], F32),
            ("dbg_p", [CH, TOK], F16), ("dbg_cc", [CH, 4], F32),
            ("dbg_haug", [33, TOK], F16), ("dbg_ln2", [128, 8, D], F32),
            ("dbg_s", [128, 8, D], F32),
            ("dbg_wtot", [128, 8, D], F16), ("dbg_k2w", [2, K64], F16),
            ("dbg_rowsa", [2, 1024], F16), ("dbg_rowsb", [1, 1024], F16),
            ("dbg_corrw", [2, D], F16),
            ("dbg_mvb", [128, NT, 2], F32), ("dbg_mvs", [128, NT, 2], F32),
            ("dbg_mysum", [CH, 6], F32), ("dbg_carryops", [CH, 6], F32),
            ("dbg_inv3", [128, NT], F32), ("dbg_nm3i", [128, NT], F32),
        ]:
            dbg[nm] = nc.dram_tensor(nm, shape, dt, kind="ExternalOutput").ap()

    ssum = nc.dram_tensor("ssum", [2, 192], F32, kind="Internal",
                          addr_space="Shared").ap()
    sflg = nc.dram_tensor("sflg", [2, 1], I32, kind="Internal",
                          addr_space="Shared").ap()

    ops = {"sync": [], "scalar": [], "vector": [], "tensor": [], "gpsimd": []}
    seqno = [0]
    import sys as _sys
    seqlog = []

    def em(eng, fn):
        fr = _sys._getframe(1)
        ln = fr.f_lineno
        fr2 = fr.f_back
        ln2 = fr2.f_lineno if fr2 else 0
        seqlog.append((seqno[0], eng, ln, ln2))
        ops[eng].append((seqno[0], fn))
        seqno[0] += 1

    c = {"qx": 0, "qw": 0, "qo": 0, "qg": 0, "sv": 0, "ss": 0, "tt": 0, "gg": 0}

    def inc(name, amt):
        c[name] += amt
        return c[name]

    with ExitStack() as ctx:
        sb = lambda name, shape, dt: ctx.enter_context(nc.sbuf_tensor(name, shape, dt))
        xt_own = sb("xt_own", [128, NT, D], F32)
        xt_l = sb("xt_l", [128, 2, D], F32)
        xfm = sb("xfm", [128, 8, 1024], F16)
        ident = sb("identt", [128, 128], F32)
        wcat32 = sb("wcat32", [128, 8, K64], F32)
        wcat16 = sb("wcat16", [128, 8, K64], F16)
        gb32 = sb("gb32", [128, 8, 2], F32)
        gb16 = sb("gb16", [128, 8, 2], F16)
        bcat2t = sb("bcat2t", [K64, 2], F32)
        k2wa = sb("k2wa", [K64, 2], F32)
        k2w16 = sb("k2w16", [2, K64], F16)
        woa16 = sb("woa16", [33, D], F16)
        stage32 = sb("stage32", [128, D], F32)
        wtot16 = sb("wtot16", [128, 8, D], F16)
        corrw16 = sb("corrw16", [2, D], F16)
        ld16 = sb("ld16", [100, 4], F16)
        da16 = sb("da16", [100, D], F16)
        lu16 = sb("lu16", [4, 50], F16)
        up16 = sb("up16", [50, D], F16)
        ld32 = sb("ld32", [100, 4], F32)
        lu32 = sb("lu32", [4, 50], F32)
        r1_16 = sb("r1_16", [4, D], F16)
        r2_16 = sb("r2_16", [50, D], F16)
        ones16 = sb("ones16", [128, 1], F16)
        st6t = sb("st6t", [128, 3, 12], F32)
        mvA = sb("mvA", [128, NT, 2], F32)
        corrA = sb("corrA", [128, NT, 3], F32)
        rows16a = sb("rows16a", [2, 1024], F16)
        rows16b = sb("rows16b", [1, 1024], F16)
        inv1rep = sb("inv1rep", [K64, 1024], F16)
        pre = sb("pre", [K64, TOK], F16)
        preB = sb("preB", [CH, TOK], F16)
        aT = sb("aT", [CH, TOK], F16)
        uo = sb("uo", [CH, TOK], F32)
        po = sb("po", [CH, TOK], F16)
        haug = sb("haug", [33, TOK], F16)
        mysum = sb("mysum", [CH, 6], F32)
        carryops = sb("carryops", [CH, 6], F32)
        ccbuf = sb("ccbuf", [CH, 4], F32)
        rolet = sb("rolet", [1, 8], I32)
        flg = sb("flg", [1, 1], I32)
        flgin = sb("flgin", [1, 1], I32)
        buf8 = sb("buf8", [128, 8, D], F32)
        l2fm = sb("l2fm", [128, 2, 1024], F16)
        mvB2 = sb("mvB2", [128, NT, 2], F32)
        mvB2b = sb("mvB2b", [128, NT, 2], F32)
        mvS2 = sb("mvS2", [128, NT, 2], F32)
        corrL16 = sb("corrL16", [2, 256], F16)
        inv3c = sb("inv3c", [128, NT], F32)
        nm3i = sb("nm3i", [128, NT], F32)
        scr16 = sb("scr16", [128, NT], F32)
        scr16b = sb("scr16b", [128, NT], F32)
        outbuf = sb("outbuf", [128, 2, D], F32)
        epsc = sb("epsc", [128, 2], F32)

        PA = ctx.enter_context(nc.psum_tensor("PA", [128, 1024], F32))
        PB = ctx.enter_context(nc.psum_tensor("PB", [128, 1024], F32))
        PC = ctx.enter_context(nc.psum_tensor("PC", [128, 1024], F32))
        PD = ctx.enter_context(nc.psum_tensor("PD", [128, 1024], F32))

        sem = {}
        for name in ("qx", "qw", "qo", "qg", "sv", "ss", "tt", "gg"):
            sem[name] = ctx.enter_context(nc.semaphore(name))

        r_my = ctx.enter_context(nc.gpsimd.register("r_my"))
        r_fl = ctx.enter_context(nc.gpsimd.register("r_fl"))
        r_k = ctx.enter_context(nc.gpsimd.register("r_k"))
        r_go = ctx.enter_context(nc.gpsimd.register("r_go"))
        r_o1 = ctx.enter_context(nc.gpsimd.register("r_o1"))

        block = ctx.enter_context(nc.Block())

        def W(eng, sname, val):
            em(eng, lambda e, s=sem[sname], v=val: e.wait_ge(s, v))

        def DMA(eng, qname, out_ap, in_ap):
            v = inc(qname, 16)
            em(eng, lambda e, o=out_ap, i=in_ap, s=sem[qname]:
               e.dma_start(out=o, in_=i).then_inc(s, 16))
            return v

        def VSP():
            VN(lambda e: e.tensor_copy(out=scr16[:, 0:1], in_=epsc[:, 0:1]))

        def CSP():
            return CI(lambda e: e.activation(out=scr16[:, 1:2], in_=epsc[:, 0:1],
                                             func=AF.Copy))

        def TI(fn):
            v = inc("tt", 1)
            em("tensor", lambda e, f=fn, s=sem["tt"]: f(e).then_inc(s, 1))
            return v

        def TN(fn):
            em("tensor", lambda e, f=fn: f(e))

        def VI(fn):
            v = inc("sv", 1)
            em("vector", lambda e, f=fn, s=sem["sv"]: f(e).then_inc(s, 1))
            return v

        def VN(fn):
            em("vector", lambda e, f=fn: f(e))

        def CI(fn):
            v = inc("ss", 1)
            em("scalar", lambda e, f=fn, s=sem["ss"]: f(e).then_inc(s, 1))
            return v

        xo_r = x_own.rearrange("(a p) f -> p a f", p=128)
        xl_r = x_lite.rearrange("(a p) f -> p a f", p=128)
        for T in range(NST):
            DMA("sync", "qx", xt_own[:, 4 * T:4 * T + 4, :], xo_r[:, 4 * T:4 * T + 4, :])
        DMA("scalar", "qw", ident[:, :], ident_in[:, :])
        DMA("scalar", "qw", rolet[:, :], rolei[:, :])
        wc_r = wcatT.rearrange("(c p) k -> p c k", p=128)
        DMA("scalar", "qw", wcat32[:, :, :], wc_r[:, :, :])
        DMA("scalar", "qw", gb32[:, :, :], gb8[:, :, :])
        DMA("scalar", "qw", bcat2t[:, :], bcat2[:, :])
        DMA("scalar", "qw", ld32[:, :], ldownT[:, :])
        DMA("scalar", "qw", lu32[:, :], lupT[:, :])
        w_all = c["qw"]

        W("vector", "qw", w_all)
        VN(lambda e: e.memset(ones16[:, :], 1.0))
        VN(lambda e: e.memset(epsc[:, 0:1], EPS))
        VN(lambda e: e.memset(epsc[:, 1:2], EPS * EPS))
        VN(lambda e: e.memset(haug[32:33, :], 1.0))
        VN(lambda e: e.memset(mysum[:, 4:5], 1.0))
        VN(lambda e: e.memset(l2fm[0:1, 0, 0:K64], 1.0))
        VN(lambda e: e.memset(mysum[:, 5:6], 0.0))
        VN(lambda e: e.tensor_copy(out=wcat16[:, :, :], in_=wcat32[:, :, :]))
        VN(lambda e: e.tensor_copy(out=gb16[:, :, :], in_=gb32[:, :, :]))
        VN(lambda e: e.tensor_copy(out=ld16[:, :], in_=ld32[:, :]))
        v_casts = VI(lambda e: e.tensor_copy(out=lu16[:, :], in_=lu32[:, :]))
        d1 = DMA("scalar", "qw", stage32[0:33, :], woutaug[:, :])
        W("vector", "qw", d1)
        v1s = VI(lambda e: e.tensor_copy(out=woa16[:, :], in_=stage32[0:33, :]))
        W("scalar", "sv", v1s)
        d2 = DMA("scalar", "qw", stage32[64:65, :], bf[:, :])
        W("gpsimd", "qw", d2)
        g_bf = inc("gg", 1)
        em("gpsimd", lambda e, s=sem["qg"]: e.dma_start(
            out=corrw16[1:2, :], in_=stage32[64:65, :]).then_inc(s, 16))
        gq_bf = c["qg"] = c["qg"] + 16
        em("gpsimd", lambda e, v=gq_bf: e.wait_ge(sem["qg"], v))
        em("gpsimd", lambda e, s=sem["gg"]: e.sem_inc(s, 1))
        v2s = c["sv"]
        W("scalar", "gg", g_bf)
        d3 = DMA("scalar", "qw", stage32[0:100, :], daux[:, :])
        W("vector", "qw", d3)
        v3s = VI(lambda e: e.tensor_copy(out=da16[:, :], in_=stage32[0:100, :]))
        W("scalar", "sv", v3s)
        d4 = DMA("scalar", "qw", stage32[0:50, :], upT[:, :])
        W("vector", "qw", d4)
        v_casts = VI(lambda e: e.tensor_copy(out=up16[:, :], in_=stage32[0:50, :]))


        W("tensor", "sv", v_casts)
        for ch in range(7):
            TN(lambda e, ch=ch: e.matmul(PD[0:K64, 0:2], wcat16[:, ch, :],
                                         gb16[:, ch, :], start=(ch == 0), stop=False))
        t_mv = TI(lambda e: e.matmul(PD[0:K64, 0:2], wcat16[:, 7, :],
                                     gb16[:, 7, :], start=False, stop=True))
        W("vector", "tt", t_mv)
        for ch in range(7):
            VN(lambda e, ch=ch: e.tensor_scalar(
                out=wcat16[:, ch, :], in0=wcat32[:, ch, :],
                scalar1=gb32[:, ch, 0:1], scalar2=None, op0=OP.mult))
        VN(lambda e: e.tensor_scalar(
            out=wcat16[:, 7, :], in0=wcat32[:, 7, :],
            scalar1=gb32[:, 7, 0:1], scalar2=None, op0=OP.mult))
        VN(lambda e: e.scalar_tensor_tensor(
            out=k2wa[:, :], in0=bcat2t[:, :], scalar=1.0, in1=PD[0:K64, 0:2],
            op0=OP.mult, op1=OP.add))
        v_k2 = VI(lambda e: e.tensor_copy(out=scr16[:, 6:7], in_=epsc[:, 0:1]))
        W("tensor", "sv", v_k2)
        W("tensor", "qw", w_all)
        t_k2 = TI(lambda e: e.transpose(PD[0:2, 4:68], k2wa[:, :], ident[0:K64, 0:K64]))
        W("scalar", "tt", t_k2)
        s_k2 = CI(lambda e: e.activation(out=k2w16[:, :], in_=PD[0:2, 4:68], func=AF.Copy))

        def alpha_phase(xtile, dma_for_tile, scans_out, sumcol, tag):
            uT, pT = scans_out
            pend = [None]
            evA = {}
            evac_mark = {}
            prevpre = [0]
            def emit_tail(Tst):
                s_evac = evac_mark[Tst]
                if pend[0] is not None:
                    VN(pend[0])
                    pend[0] = None
                VSP()
                g = slice(4 * Tst, 4 * Tst + 4)
                v_var = VI(lambda e, g=g: e.tensor_scalar(
                    out=corrA[:, g, 0], in0=mvA[:, g, 0], scalar1=-1.0, scalar2=None,
                    op0=OP.mult))
                W("scalar", "sv", v_var)
                CI(lambda e, g=g: e.activation(
                    out=corrA[:, g, 1], in_=mvA[:, g, 1], func=AF.Sqrt, bias=epsc[:, 0:1], scale=1.0))
                s_sd = CSP()
                W("vector", "ss", s_sd)
                VN(lambda e, g=g: e.reciprocal(out=corrA[:, g, 2], in_=corrA[:, g, 1]))
                v_inv = VI(lambda e: e.tensor_copy(out=scr16[:, 6:7], in_=epsc[:, 0:1]))
                W("tensor", "sv", v_inv)
                for j in range(4):
                    t = 4 * Tst + j
                    TN(lambda e, t=t, j=j: e.transpose(
                        PD[0:2, 128 * j:128 * j + 128], corrA[:, t, 0:2], ident[:, :]))
                    fn = lambda e, t=t, j=j: e.transpose(
                        PD[0:1, 512 + 128 * j:512 + 128 * j + 128], corrA[:, t, 2:3],
                        ident[:, :])
                    if j < 3:
                        TN(fn)
                    else:
                        t_tiny = TI(fn)
                ro = 512 * (Tst % 2)
                W("scalar", "tt", t_tiny)
                W("scalar", "sv", c["sv"])
                CI(lambda e, ro=ro: e.activation(
                    out=rows16a[:, ro:ro + 512],
                    in_=PD[0:2, 0:512], func=AF.Copy))
                CI(lambda e, ro=ro: e.activation(
                    out=rows16b[:, ro:ro + 512],
                    in_=PD[0:1, 512:1024], func=AF.Copy))
                s_rows = CSP()
                W("tensor", "ss", s_evac)
                W("tensor", "ss", s_rows)
                W("tensor", "sv", c["sv"])
                sl = slice(512 * Tst, 512 * Tst + 512)
                par = Tst % 2
                xo = 512 * (Tst % 2)
                for ch in range(8):
                    TN(lambda e, ch=ch, xo=xo: e.matmul(
                        PC[0:K64, 0:512], wcat16[:, ch, :], xfm[:, ch, xo:xo + 512],
                        start=(ch == 0), stop=False))
                t_proj = TI(lambda e, ro=ro: e.matmul(
                    PC[0:K64, 0:512], k2w16[:, :], rows16a[0:2, ro:ro + 512],
                    start=False, stop=True))
                t_rep = TI(lambda e, ro=ro: e.matmul(
                    PC[0:K64, 512:1024], l2fm[0:1, 0, 0:K64], rows16b[0:1, ro:ro + 512],
                    start=True, stop=True))
                prevpre[0] = t_rep
                W("scalar", "tt", t_rep)
                CI(lambda e, ro=ro: e.activation(
                    out=inv1rep[:, ro:ro + 512], in_=PC[0:K64, 512:1024], func=AF.Copy))
                s_irep = CSP()
                W("vector", "tt", t_proj)
                W("vector", "ss", s_irep)
                VN(lambda e, sl=sl, ro=ro: e.tensor_tensor(
                    out=pre[:, sl], in0=inv1rep[:, ro:ro + 512], in1=PC[0:K64, 0:512],
                    op=OP.mult))
                v_pre = VI(lambda e: e.tensor_copy(out=scr16[:, 6:7], in_=epsc[:, 0:1]))

            for Tst in range(NST):
                for j in range(4):
                    t = 4 * Tst + j
                    PS = PA if t % 2 == 0 else PB
                    slot = t if xtile is xt_own else (t % 2)
                    xdma_v = dma_for_tile(t)
                    W("vector", "qx", xdma_v)
                    VN(lambda e, slot=slot, t=t: e.bn_stats(
                        out=st6t[:, t % 2, 0:6], in_=xtile[:, slot, 0:512]))
                    VN(lambda e, slot=slot, t=t: e.bn_stats(
                        out=st6t[:, t % 2, 6:12], in_=xtile[:, slot, 512:1024]))
                    if pend[0] is not None:
                        VN(pend[0])
                    pend[0] = (lambda e, t=t: e.bn_aggr(
                        out=mvA[:, t, :], in_=st6t[:, t % 2, :]))
                    W("tensor", "qx", xdma_v)
                    if t >= 2:
                        W("tensor", "ss", evA[t - 2])
                    for ch in range(8):
                        bank_col = 512 * (ch // 4) + 128 * (ch % 4)
                        fn = lambda e, ch=ch, slot=slot, PS=PS, bank_col=bank_col: \
                            e.transpose(PS[:, bank_col:bank_col + 128],
                                        xtile[:, slot, 128 * ch:128 * ch + 128], ident[:, :])
                        if ch < 7:
                            TN(fn)
                        else:
                            tv = TI(fn)
                    W("scalar", "tt", tv)
                    if Tst > 0:
                        W("scalar", "tt", prevpre[0])
                    xo = 512 * (Tst % 2)
                    for hh in range(2):
                        CI(lambda e, hh=hh, j=j, PS=PS, xo=xo: e.activation(
                            out=xfm[:, 4 * hh:4 * hh + 4, xo + 128 * j:xo + 128 * j + 128],
                            in_=PS[:, 512 * hh:512 * hh + 512].rearrange(
                                "p (c q) -> p c q", q=128),
                            func=AF.Copy))
                    evA[t] = c["ss"]
                CSP()
                evac_mark[Tst] = c["ss"]
                emit_tail(Tst)
            W("scalar", "sv", c["sv"])
            CI(lambda e: e.activation(
                out=aT[:, :], in_=pre[0:CH, :], func=AF.Sigmoid))
            s_sig = CSP()
            W("sync", "sv", c["sv"])
            d_pb = DMA("sync", "qx", preB[:, :], pre[CH:K64, :])
            W("vector", "ss", s_sig)
            W("vector", "qx", d_pb)
            VN(lambda e, uT=uT: e.tensor_tensor_scan(
                out=uT[:, :], data0=aT[:, :], data1=preB[:, :],
                initial=0.0, op0=OP.mult, op1=OP.add))
            VN(lambda e, pT=pT: e.tensor_tensor_scan(
                out=pT[:, :], data0=aT[:, :], data1=aT[:, :],
                initial=1.0, op0=OP.mult, op1=OP.bypass))
            VN(lambda e, uT=uT, sc=sumcol: e.tensor_copy(
                out=mysum[:, sc + 1:sc + 2], in_=uT[:, TOK - 1:TOK]))
            VN(lambda e, pT=pT, sc=sumcol: e.tensor_copy(
                out=mysum[:, sc:sc + 1], in_=pT[:, TOK - 1:TOK]))
            VSP()
            return VI(lambda e: e.tensor_copy(out=scr16[:, 6:7], in_=epsc[:, 0:1]))

        def ddump(name, src_ap):
            if DBG:
                W("sync", "sv", c["sv"])
                W("sync", "ss", c["ss"])
                W("sync", "tt", c["tt"])
                W("sync", "gg", c["gg"])
                v = DMA("sync", "qo", dbg[name], src_ap)
                if DBG == 1:
                    for eng in ("vector", "scalar", "tensor"):
                        W(eng, "qo", v)

        ddump("dbg_k2w", k2w16[:, :])

        v_own = alpha_phase(xt_own, lambda t: 16 * (t // 4 + 1), (uo, po), 0, "own")

        print("SEQMARK ownA_done", seqno[0])
        ddump("dbg_mva", mvA[:, :, :])
        ddump("dbg_pre", pre[:, :])
        ddump("dbg_a", aT[:, :])
        ddump("dbg_u", uo[:, :])
        ddump("dbg_p", po[:, :])
        ddump("dbg_rowsa", rows16a[:, :])
        ddump("dbg_rowsb", rows16b[:, :])

        lite_q = {}

        def lite_dma(t):
            if t not in lite_q:
                if t >= 2:
                    W("sync", "tt", c["tt"])
                    W("sync", "sv", c["sv"])
                lite_q[t] = DMA("sync", "qx", xt_l[:, t % 2:t % 2 + 1, :],
                                xl_r[:, t:t + 1, :])
            return lite_q[t]

        lite_dma(0)
        lite_dma(1)
        v_lite = alpha_phase(xt_l, lite_dma, (pre[0:CH, :], preB[:, :]), 2, "lite")

        print("SEQMARK liteA_done", seqno[0])
        W("gpsimd", "sv", v_lite)
        em("gpsimd", lambda e: e.reg_load(r_my, rolet[:1, 5:6]))
        gd1 = inc("qg", 16)
        em("gpsimd", lambda e, s=sem["qg"]: e.dma_start(
            bass.AP(ssum.tensor, r_my, [[6, CH], [1, 6]]), mysum[:, :]).then_inc(s, 16))
        em("gpsimd", lambda e, v=gd1: e.wait_ge(sem["qg"], v))
        em("gpsimd", lambda e: e.reg_load(r_fl, rolet[:1, 6:7]))
        W("vector", "qw", w_all)
        VN(lambda e: e.memset(flg[:, :], FLAGV))
        v_flg = VI(lambda e: e.tensor_copy(out=scr16[:, 6:7], in_=epsc[:, 0:1]))
        W("gpsimd", "sv", v_flg)
        gd2 = inc("qg", 16)
        em("gpsimd", lambda e, s=sem["qg"]: e.dma_start(
            bass.AP(sflg.tensor, r_fl, [[1, 1], [1, 1], [1, 1]]), flg[0:1, :]).then_inc(s, 16))
        em("gpsimd", lambda e, v=gd2: e.wait_ge(sem["qg"], v))
        em("gpsimd", lambda e: e.reg_load(r_o1, rolet[:1, 7:8]))
        em("gpsimd", lambda e: e.reg_mov(r_k, gd2))
        em("gpsimd", lambda e: e.reg_mov(r_go, 1))

        def poll_body(e):
            guard = e.While(r_go)
            guard.__enter__()
            e.dma_start(flgin[0:1, :],
                        bass.AP(sflg.tensor, r_o1, [[1, 1], [1, 1], [1, 1]])).then_inc(sem["qg"], 16)
            e.reg_add(r_k, r_k, 16)
            e.wait_ge(sem["qg"], r_k)
            e.reg_load(r_fl, flgin[:1, :1])
            ig = e.If_eq(r_fl, FLAGV)
            ig.__enter__()
            e.reg_mov(r_go, 0)
            ig.__exit__(None, None, None)
            guard.__exit__(None, None, None)
        em("gpsimd", poll_body)
        for j in range(3):
            em("gpsimd", lambda e, j=j: e.reg_load(r_o1, rolet[:1, 2 + j:3 + j]))
            em("gpsimd", lambda e, j=j, s=sem["qg"]: e.dma_start(
                carryops[:, 2 * j:2 * j + 2],
                bass.AP(ssum.tensor, r_o1, [[6, CH], [1, 2]])).then_inc(s, 16))
        em("gpsimd", lambda e: e.reg_add(r_k, r_k, 48))
        em("gpsimd", lambda e: e.wait_ge(sem["qg"], r_k))
        g_done = inc("gg", 1)
        em("gpsimd", lambda e, s=sem["gg"]: e.sem_inc(s, 1))
        W("vector", "gg", g_done)
        VN(lambda e: e.tensor_copy(out=ccbuf[:, 0:1], in_=carryops[:, 1:2]))
        VSP()
        VN(lambda e: e.scalar_tensor_tensor(
            out=ccbuf[:, 1:2], in0=carryops[:, 2:3], scalar=ccbuf[:, 0:1],
            in1=carryops[:, 3:4], op0=OP.mult, op1=OP.add))
        VSP()
        VN(lambda e: e.scalar_tensor_tensor(
            out=ccbuf[:, 2:3], in0=carryops[:, 4:5], scalar=ccbuf[:, 1:2],
            in1=carryops[:, 5:6], op0=OP.mult, op1=OP.add))
        VSP()
        VN(lambda e: e.scalar_tensor_tensor(
            out=haug[0:CH, :], in0=po[:, :], scalar=ccbuf[:, 2:3], in1=uo[:, :],
            op0=OP.mult, op1=OP.add))
        VSP()
        v_h = VI(lambda e: e.tensor_copy(out=scr16[:, 3:4], in_=epsc[:, 0:1]))

        print("SEQMARK exchange_done", seqno[0])
        ddump("dbg_mysum", mysum[:, :])
        ddump("dbg_carryops", carryops[:, :])
        ddump("dbg_cc", ccbuf[:, :])
        ddump("dbg_haug", haug[:, :])

        TN(lambda e: e.matmul(PA[0:4, 0:512], ld16[:, :], da16[:, 0:512],
                              start=True, stop=True))
        t_r1 = TI(lambda e: e.matmul(PA[0:4, 512:1024], ld16[:, :], da16[:, 512:1024],
                                     start=True, stop=True))
        W("scalar", "tt", t_r1)
        CI(lambda e: e.activation(out=r1_16[:, :], in_=PA[0:4, 0:1024], func=AF.Copy))
        s_r1 = CSP()

        W("tensor", "ss", s_r1)
        TN(lambda e: e.matmul(PB[0:50, 0:512], lu16[:, :], r1_16[:, 0:512],
                              start=True, stop=True))
        t_r2 = TI(lambda e: e.matmul(PB[0:50, 512:1024], lu16[:, :], r1_16[:, 512:1024],
                                     start=True, stop=True))
        W("scalar", "tt", t_r2)
        CI(lambda e: e.activation(out=r2_16[:, :], in_=PB[0:50, 0:1024], func=AF.Copy))
        s_r2 = CSP()
        wft_r = wfT.rearrange("(c p) o -> p c o", p=128)
        for ci in range(8):
            if ci == 0:
                W("scalar", "sv", v_casts)
            dv = DMA("scalar", "qw", stage32[:, :], wft_r[:, ci, :])
            W("tensor", "ss", s_r2)
            W("tensor", "sv", c["sv"])
            TN(lambda e, ci=ci: e.matmul(
                PC[:, 0:512], r2_16[:, 128 * ci:128 * ci + 128],
                up16[:, 0:512], start=True, stop=True))
            tv = TI(lambda e, ci=ci: e.matmul(
                PC[:, 512:1024], r2_16[:, 128 * ci:128 * ci + 128],
                up16[:, 512:1024], start=True, stop=True))
            W("vector", "qw", dv)
            W("vector", "tt", tv)
            vv = VI(lambda e, ci=ci: e.scalar_tensor_tensor(
                out=wtot16[:, ci, :], in0=stage32[:, :], scalar=1.0, in1=PC[:, :],
                op0=OP.mult, op1=OP.add))
            W("scalar", "sv", vv)
        VSP()
        v_wtot = VI(lambda e: e.tensor_copy(out=scr16[:, 6:7], in_=epsc[:, 0:1]))
        W("tensor", "sv", v_wtot)
        for h in range(2):
            for ci in range(8):
                fn = lambda e, ci=ci, h=h: e.matmul(
                    PD[0:1, 512 * h:512 * h + 512], ones16[:, :],
                    wtot16[:, ci, 512 * h:512 * h + 512], start=(ci == 0), stop=(ci == 7))
                if h == 1 and ci == 7:
                    t_s2 = TI(fn)
                else:
                    TN(fn)
        W("vector", "tt", t_s2)
        VN(lambda e: e.tensor_scalar(out=corrw16[0:1, :], in0=PD[0:1, 0:1024],
                                     scalar1=-1.0, scalar2=-1.0,
                                     op0=OP.mult, op1=OP.add))
        VSP()
        v_cw = VI(lambda e: e.tensor_copy(out=scr16[:, 5:6], in_=epsc[:, 0:1]))

        print("SEQMARK P2_done", seqno[0])


        g_weights = c["gg"]
        ddump("dbg_wtot", wtot16[:, :, :])
        ddump("dbg_corrw", corrw16[:, :])
        store_v = {}
        outp_v = {}
        v2_v = {}
        v4_v = {}
        l2ev = {}
        sqv = {}
        sclv = {}
        pend_aggr3 = [None]
        for half in range(2):
            tiles = range(8 * half, 8 * half + 8)
            if half == 0:
                W("tensor", "sv", v_h)
                W("tensor", "sv", v_cw)
                W("tensor", "gg", g_weights)

            for t in tiles:
                if t - 1 in v2_v:
                    W("tensor", "sv", v2_v[t - 1])
                for hh in range(2):
                    fn = lambda e, t=t, hh=hh: e.matmul(
                        PD[:, 512 * hh:512 * hh + 512], haug[:, 128 * t:128 * t + 128],
                        woa16[:, 512 * hh:512 * hh + 512], start=True, stop=True)
                    if hh == 0:
                        TN(fn)
                    else:
                        t_wo = TI(fn)
                if half == 1 and (t % 8) in outp_v:
                    W("vector", "ss", outp_v[t % 8])
                W("vector", "tt", t_wo)
                VN(lambda e, t=t: e.scalar_tensor_tensor(
                    out=buf8[:, t % 8, :], in0=xt_own[:, t, :], scalar=2.0,
                    in1=PD[:, :], op0=OP.mult, op1=OP.add))
                v2_v[t] = VI(lambda e: e.tensor_copy(out=scr16[:, 4:5], in_=epsc[:, 0:1]))
                VN(lambda e, t=t: e.bn_stats(out=st6t[:, 0, 0:6],
                                             in_=buf8[:, t % 8, 0:512]))
                VN(lambda e, t=t: e.bn_stats(out=st6t[:, 0, 6:12],
                                             in_=buf8[:, t % 8, 512:1024]))
                VSP()
                VN(lambda e, t=t: e.bn_aggr(out=mvB2[:, t, :], in_=st6t[:, 0, :]))
                VSP()
                v_ag2 = VI(lambda e: e.tensor_copy(out=scr16[:, 6:7],
                                                   in_=epsc[:, 0:1]))
                W("scalar", "sv", v_ag2)
                CI(lambda e, t=t: e.activation(
                    out=mvB2[:, t, 1:2], in_=mvB2[:, t, 1:2], func=AF.Sqrt,
                    bias=epsc[:, 0:1], scale=1.0))
                sqv[t] = CSP()
                P_l = PA if t % 2 == 0 else PB
                if t - 2 in l2ev:
                    W("tensor", "ss", l2ev[t - 2])
                W("tensor", "sv", v2_v[t])
                for ch in range(8):
                    bank_col = 512 * (ch // 4) + 128 * (ch % 4)
                    fn = lambda e, ch=ch, t=t, bank_col=bank_col, P_l=P_l: e.transpose(
                        P_l[:, bank_col:bank_col + 128],
                        buf8[:, t % 8, 128 * ch:128 * ch + 128], ident[:, :])
                    if ch < 7:
                        TN(fn)
                    else:
                        t_l2 = TI(fn)
                W("tensor", "ss", sqv[t])
                if t - 1 in v4_v:
                    W("tensor", "sv", v4_v[t - 1])
                t_tb = TI(lambda e, t=t: e.transpose(
                    PC[0:2, 0:128], mvB2[:, t, :], ident[:, :]))
                W("scalar", "tt", t_l2)
                for hh in range(2):
                    CI(lambda e, hh=hh, t=t, P_l=P_l: e.activation(
                        out=l2fm[:, t % 2, 512 * hh:512 * hh + 512],
                        in_=P_l[:, 512 * hh:512 * hh + 512],
                        func=AF.Copy))
                l2ev[t] = c["ss"]
                W("scalar", "tt", t_tb)
                CI(lambda e, t=t: e.activation(
                    out=corrL16[:, 128 * (t % 2):128 * (t % 2) + 128],
                    in_=PC[0:2, 0:128], func=AF.Copy))
                sclv[t] = CSP()
                W("tensor", "ss", sclv[t])
                for hh in range(2):
                    for ci in range(8):
                        TN(lambda e, ci=ci, hh=hh, t=t: e.matmul(
                            PC[:, 512 * hh:512 * hh + 512],
                            l2fm[:, t % 2, 128 * ci:128 * ci + 128],
                            wtot16[:, ci, 512 * hh:512 * hh + 512],
                            start=(ci == 0), stop=False))
                    fn = lambda e, hh=hh, t=t: e.matmul(
                        PC[:, 512 * hh:512 * hh + 512],
                        corrL16[:, 128 * (t % 2):128 * (t % 2) + 128],
                        corrw16[:, 512 * hh:512 * hh + 512], start=False, stop=True)
                    if hh == 0:
                        TN(fn)
                    else:
                        t_ffn = TI(fn)
                W("vector", "tt", t_ffn)
                v4_v[t] = VI(lambda e, t=t: e.tensor_tensor(
                    out=buf8[:, t % 8, :], in0=buf8[:, t % 8, :], in1=PC[:, :],
                    op=OP.add))
                if t % 8 > 0:
                    tp = t - 1
                    sl3 = 1 + (tp % 2)
                    VN(lambda e, tp=tp, sl3=sl3: e.bn_stats(out=st6t[:, sl3, 0:6],
                                                            in_=buf8[:, tp % 8, 0:512]))
                    VN(lambda e, tp=tp, sl3=sl3: e.bn_stats(out=st6t[:, sl3, 6:12],
                                                            in_=buf8[:, tp % 8, 512:1024]))
                    if pend_aggr3[0] is not None:
                        VN(pend_aggr3[0])
                    pend_aggr3[0] = (lambda e, tp=tp, sl3=sl3: e.bn_aggr(
                        out=mvS2[:, tp, :], in_=st6t[:, sl3, :]))
            if half == 0:
                ddump("dbg_ln2", buf8[:, :, :])
            tl = 8 * half + 7
            sl3t = 1 + (tl % 2)
            VSP()
            VN(lambda e, tl=tl, sl3t=sl3t: e.bn_stats(out=st6t[:, sl3t, 0:6],
                                                      in_=buf8[:, tl % 8, 0:512]))
            VN(lambda e, tl=tl, sl3t=sl3t: e.bn_stats(out=st6t[:, sl3t, 6:12],
                                                      in_=buf8[:, tl % 8, 512:1024]))
            if pend_aggr3[0] is not None:
                VN(pend_aggr3[0])
                pend_aggr3[0] = None
            VSP()
            VN(lambda e, tl=tl, sl3t=sl3t: e.bn_aggr(out=mvS2[:, tl, :], in_=st6t[:, sl3t, :]))
            VSP()
            g = slice(8 * half, 8 * half + 8)
            W("vector", "ss", sqv[8 * half + 7])
            VN(lambda e, g=g: e.tensor_tensor(
                out=scr16b[:, g], in0=mvB2[:, g, 1], in1=mvB2[:, g, 1], op=OP.mult))
            VSP()
            VN(lambda e, g=g: e.scalar_tensor_tensor(
                out=scr16b[:, g], in0=scr16b[:, g], scalar=EPS, in1=mvS2[:, g, 1],
                op0=OP.mult, op1=OP.add))
            v_v3 = VI(lambda e: e.tensor_copy(out=scr16[:, 6:7], in_=epsc[:, 0:1]))
            W("scalar", "sv", v_v3)
            CI(lambda e, g=g: e.activation(
                out=nm3i[:, g], in_=scr16b[:, g], func=AF.Sqrt, bias=0.0, scale=1.0))
            s_sd3 = CSP()
            W("vector", "ss", s_sd3)
            VN(lambda e, g=g: e.reciprocal(out=inv3c[:, g], in_=nm3i[:, g]))
            VSP()
            VN(lambda e, g=g: e.tensor_tensor(
                out=scr16b[:, g], in0=mvS2[:, g, 0], in1=inv3c[:, g], op=OP.mult))
            VSP()
            VN(lambda e, g=g: e.tensor_scalar(
                out=nm3i[:, g], in0=scr16b[:, g], scalar1=-1.0, scalar2=None, op0=OP.mult))
            VSP()
            v_nm = VI(lambda e: e.tensor_copy(out=scr16[:, 2:3], in_=epsc[:, 0:1]))
            if half == 0:
                ddump("dbg_s", buf8[:, :, :])
                ddump("dbg_mvb", mvB2[:, :, :])
                ddump("dbg_mvs", mvS2[:, :, :])

            if half == 0:
                ddump("dbg_inv3", inv3c[:, :])
                ddump("dbg_nm3i", nm3i[:, :])
            W("scalar", "sv", v_nm)
            for t in tiles:
                if t >= 2:
                    W("scalar", "qo", store_v[t // 2 - 1])
                outp_v[t % 8] = CI(lambda e, t=t: e.activation(
                    out=outbuf[:, t % 2, :], in_=buf8[:, t % 8, :], func=AF.Identity,
                    scale=inv3c[:, t:t + 1], bias=nm3i[:, t:t + 1]))
                if t % 2 == 1:
                    CSP()
                    W("sync", "ss", c["ss"])
                    store_v[t // 2] = DMA(
                        "sync", "qo",
                        y.rearrange("(a p) f -> p a f", p=128)[:, t - 1:t + 1, :],
                        outbuf[:, :, :])
        W("sync", "qo", c["qo"])

        import os
        cut = int(os.environ.get("KMK_CUT", "0")) or None

        def run_list(eng_obj, lst):
            for s, f in lst:
                if cut is not None and s >= cut:
                    break
                f(eng_obj)

        @block.sync
        def _(sync):
            run_list(sync, ops["sync"])

        @block.scalar
        def _(scalar):
            run_list(scalar, ops["scalar"])

        @block.vector
        def _(vector):
            run_list(vector, ops["vector"])

        @block.tensor
        def _(tensor):
            run_list(tensor, ops["tensor"])

        @block.gpsimd
        def _(gpsimd):
            run_list(gpsimd, ops["gpsimd"])

    if int(os.environ.get("KMK_PRINT_SEQ", "0")):
        print("total seq:", seqno[0])
        with open("/root/problem/seqmap.txt", "w") as fh:
            for row in seqlog:
                fh.write(repr(row) + "\n")
    return nc


def kernel(**inputs):
    x = np.asarray(inputs["L0_attn_x"], dtype=np.float32)
    W_a = np.asarray(inputs["L0_attn_W_a"], dtype=np.float32)
    b_a = np.asarray(inputs["L0_attn_b_a"], dtype=np.float32)
    W_in = np.asarray(inputs["L0_attn_W_in"], dtype=np.float32)
    b_in = np.asarray(inputs["L0_attn_b_in"], dtype=np.float32)
    W_out = np.asarray(inputs["L0_attn_W_out"], dtype=np.float32)
    b_out = np.asarray(inputs["L0_attn_b_out"], dtype=np.float32)
    gamma = np.asarray(inputs["L0_attn_ln_gamma"], dtype=np.float32)
    beta = np.asarray(inputs["L0_attn_ln_beta"], dtype=np.float32)
    Wf = np.asarray(inputs["L0_ffn_W"], dtype=np.float32)
    bfv = np.asarray(inputs["L0_ffn_b"], dtype=np.float32)
    lora_down = np.asarray(inputs["L0_ffn_lora_down"], dtype=np.float32)
    lora_up = np.asarray(inputs["L0_ffn_lora_up"], dtype=np.float32)
    down_aux = np.asarray(inputs["L0_ffn_down_aux"], dtype=np.float32)
    up_aux = np.asarray(inputs["L0_ffn_up_aux"], dtype=np.float32)

    nc = build_nc()

    wcatT = np.ascontiguousarray(np.concatenate([W_a, W_in], axis=0).T)
    gb8 = np.ascontiguousarray(
        np.stack([gamma.reshape(8, 128).T, beta.reshape(8, 128).T], axis=2))
    bcat2 = np.ascontiguousarray(
        np.stack([np.zeros(64, np.float32), np.concatenate([b_a, b_in])], axis=1))
    woutaug = np.ascontiguousarray(np.concatenate([W_out.T, b_out[None, :]], axis=0))
    wfT = np.ascontiguousarray(Wf.T)
    common = {
        "wcatT": wcatT, "gb8": gb8, "bcat2": bcat2, "woutaug": woutaug,
        "wfT": wfT, "bfv": np.ascontiguousarray(bfv[None, :]),
        "ldownT": np.ascontiguousarray(lora_down.T),
        "daux": np.ascontiguousarray(down_aux),
        "lupT": np.ascontiguousarray(lora_up.T),
        "upT": np.ascontiguousarray(up_aux.T),
        "ident": np.eye(128, dtype=np.float32),
    }

    in_maps = []
    for core in range(8):
        b = core // 4
        cidx = core % 4
        own = x[b, 2048 * cidx:2048 * (cidx + 1), :]
        if cidx >= 2:
            lite = x[b, 2048 * (cidx - 2):2048 * (cidx - 1), :]
        else:
            lite = own
        par = core & 1
        myrow, otrow = par, 1 - par
        ido = myrow * 192 + 4
        if cidx == 0:
            o1, o2, o3 = ido, ido, ido
        elif cidx == 1:
            o1, o2, o3 = otrow * 192 + 0, ido, ido
        elif cidx == 2:
            o1, o2, o3 = myrow * 192 + 2, otrow * 192 + 2, ido
        else:
            o1, o2, o3 = otrow * 192 + 2, myrow * 192 + 2, otrow * 192 + 0
        rolei = np.array([[par, 1 - par, o1, o2, o3, myrow * 192, myrow, otrow]],
                         dtype=np.int32)
        m = dict(common)
        m["x_own"] = np.ascontiguousarray(own)
        m["x_lite"] = np.ascontiguousarray(lite)
        m["rolei"] = rolei
        in_maps.append(m)

    res = run_bass_kernel_spmd(nc, in_maps, core_ids=list(range(8)))
    out = np.empty((2, 8192, 1024), dtype=np.float32)
    for core in range(8):
        b = core // 4
        cidx = core % 4
        out[b, 2048 * cidx:2048 * (cidx + 1), :] = res.results[core]["y"]
    return out


if __name__ == "__main__":
    build_nc()
    print("build ok")


# version 11
# speedup vs baseline: 3.5166x; 3.5166x over previous
import numpy as np
from contextlib import ExitStack
import concourse.bass as bass
import concourse.mybir as mybir
from concourse.bass_utils import run_bass_kernel_spmd

F32 = mybir.dt.float32
F16 = mybir.dt.float16
I32 = mybir.dt.int32
AF = mybir.ActivationFunctionType
OP = mybir.AluOpType
AX = mybir.AxisListType

B, S, D = 2, 8192, 1024
CH = 32
K64 = 64
TOK = 2048
NT = 16
NST = 4
EPS = 1e-5
FLAGV = 7


def build_nc():
    import os
    DBG = int(os.environ.get("KMK_DBG", "0"))
    nc = bass.Bass("TRN2", target_bir_lowering=False)

    x_own = nc.dram_tensor("x_own", [TOK, D], F32, kind="ExternalInput").ap()
    x_lite = nc.dram_tensor("x_lite", [TOK, D], F32, kind="ExternalInput").ap()
    wcatT = nc.dram_tensor("wcatT", [D, K64], F32, kind="ExternalInput").ap()
    gb8 = nc.dram_tensor("gb8", [128, 8, 2], F32, kind="ExternalInput").ap()
    bcat2 = nc.dram_tensor("bcat2", [K64, 2], F32, kind="ExternalInput").ap()
    woutaug = nc.dram_tensor("woutaug", [33, D], F32, kind="ExternalInput").ap()
    wfT = nc.dram_tensor("wfT", [D, D], F32, kind="ExternalInput").ap()
    bf = nc.dram_tensor("bfv", [1, D], F32, kind="ExternalInput").ap()
    ldownT = nc.dram_tensor("ldownT", [100, 4], F32, kind="ExternalInput").ap()
    daux = nc.dram_tensor("daux", [100, D], F32, kind="ExternalInput").ap()
    lupT = nc.dram_tensor("lupT", [4, 50], F32, kind="ExternalInput").ap()
    upT = nc.dram_tensor("upT", [50, D], F32, kind="ExternalInput").ap()
    ident_in = nc.dram_tensor("ident", [128, 128], F32, kind="ExternalInput").ap()
    rolei = nc.dram_tensor("rolei", [1, 8], I32, kind="ExternalInput").ap()
    y = nc.dram_tensor("y", [TOK, D], F32, kind="ExternalOutput").ap()

    dbg = {}
    if DBG:
        for nm, shape, dt in [
            ("dbg_mva", [128, NT, 2], F32), ("dbg_pre", [K64, TOK], F16),
            ("dbg_a", [CH, TOK], F16), ("dbg_u", [CH, TOK], F32),
            ("dbg_p", [CH, TOK], F16), ("dbg_cc", [CH, 4], F32),
            ("dbg_haug", [33, TOK], F16), ("dbg_ln2", [128, 8, D], F32),
            ("dbg_s", [128, 8, D], F32),
            ("dbg_wtot", [128, 8, D], F16), ("dbg_k2w", [2, K64], F16),
            ("dbg_rowsa", [2, 1024], F16), ("dbg_rowsb", [1, 1024], F16),
            ("dbg_corrw", [2, D], F16),
            ("dbg_mvb", [128, NT, 2], F32), ("dbg_mvs", [128, NT, 2], F32),
            ("dbg_mysum", [CH, 6], F32), ("dbg_carryops", [CH, 6], F32),
            ("dbg_inv3", [128, NT], F32), ("dbg_nm3i", [128, NT], F32),
        ]:
            dbg[nm] = nc.dram_tensor(nm, shape, dt, kind="ExternalOutput").ap()

    ssum = nc.dram_tensor("ssum", [2, 192], F32, kind="Internal",
                          addr_space="Shared").ap()
    sflg = nc.dram_tensor("sflg", [2, 1], I32, kind="Internal",
                          addr_space="Shared").ap()

    ops = {"sync": [], "scalar": [], "vector": [], "tensor": [], "gpsimd": []}
    seqno = [0]
    import sys as _sys
    seqlog = []

    def em(eng, fn):
        fr = _sys._getframe(1)
        ln = fr.f_lineno
        fr2 = fr.f_back
        ln2 = fr2.f_lineno if fr2 else 0
        seqlog.append((seqno[0], eng, ln, ln2))
        ops[eng].append((seqno[0], fn))
        seqno[0] += 1

    c = {"qx": 0, "qw": 0, "qo": 0, "qg": 0, "sv": 0, "ss": 0, "tt": 0, "gg": 0}

    def inc(name, amt):
        c[name] += amt
        return c[name]

    with ExitStack() as ctx:
        sb = lambda name, shape, dt: ctx.enter_context(nc.sbuf_tensor(name, shape, dt))
        xt_own = sb("xt_own", [128, NT, D], F32)
        xt_l = sb("xt_l", [128, 2, D], F32)
        xfm = sb("xfm", [128, 8, 1024], F16)
        ident = sb("identt", [128, 128], F32)
        wcat32 = sb("wcat32", [128, 8, K64], F32)
        wcat16 = sb("wcat16", [128, 8, K64], F16)
        gb32 = sb("gb32", [128, 8, 2], F32)
        gb16 = sb("gb16", [128, 8, 2], F16)
        bcat2t = sb("bcat2t", [K64, 2], F32)
        k2wa = sb("k2wa", [K64, 2], F32)
        k2w16 = sb("k2w16", [2, K64], F16)
        woa16 = sb("woa16", [33, D], F16)
        stage32 = sb("stage32", [128, D], F32)
        wtot16 = sb("wtot16", [128, 8, D], F16)
        corrw16 = sb("corrw16", [2, D], F16)
        ld16 = sb("ld16", [100, 4], F16)
        da16 = sb("da16", [100, D], F16)
        lu16 = sb("lu16", [4, 50], F16)
        up16 = sb("up16", [50, D], F16)
        ld32 = sb("ld32", [100, 4], F32)
        lu32 = sb("lu32", [4, 50], F32)
        r1_16 = sb("r1_16", [4, D], F16)
        r2_16 = sb("r2_16", [50, D], F16)
        ones16 = sb("ones16", [128, 1], F16)
        st6t = sb("st6t", [128, 3, 12], F32)
        mvA = sb("mvA", [128, NT, 2], F32)
        corrA = sb("corrA", [128, NT, 3], F32)
        rows16a = sb("rows16a", [2, 1024], F16)
        rows16b = sb("rows16b", [1, 1024], F16)
        inv1rep = sb("inv1rep", [K64, 1024], F16)
        pre = sb("pre", [K64, TOK], F16)
        preB = sb("preB", [CH, TOK], F16)
        aT = sb("aT", [CH, TOK], F16)
        uo = sb("uo", [CH, TOK], F32)
        po = sb("po", [CH, TOK], F16)
        haug = sb("haug", [33, TOK], F16)
        mysum = sb("mysum", [CH, 6], F32)
        carryops = sb("carryops", [CH, 6], F32)
        ccbuf = sb("ccbuf", [CH, 4], F32)
        rolet = sb("rolet", [1, 8], I32)
        flg = sb("flg", [1, 1], I32)
        flgin = sb("flgin", [1, 1], I32)
        buf8 = sb("buf8", [128, 8, D], F32)
        l2fm = sb("l2fm", [128, 2, 1024], F16)
        mvB2 = sb("mvB2", [128, NT, 2], F32)
        mvB2b = sb("mvB2b", [128, NT, 2], F32)
        mvS2 = sb("mvS2", [128, NT, 2], F32)
        corrL16 = sb("corrL16", [2, 256], F16)
        inv3c = sb("inv3c", [128, NT], F32)
        nm3i = sb("nm3i", [128, NT], F32)
        scr16 = sb("scr16", [128, NT], F32)
        scr16b = sb("scr16b", [128, NT], F32)
        outbuf = sb("outbuf", [128, 2, D], F32)
        epsc = sb("epsc", [128, 2], F32)

        PA = ctx.enter_context(nc.psum_tensor("PA", [128, 1024], F32))
        PB = ctx.enter_context(nc.psum_tensor("PB", [128, 1024], F32))
        PC = ctx.enter_context(nc.psum_tensor("PC", [128, 1024], F32))
        PD = ctx.enter_context(nc.psum_tensor("PD", [128, 1024], F32))

        sem = {}
        for name in ("qx", "qw", "qo", "qg", "sv", "ss", "tt", "gg"):
            sem[name] = ctx.enter_context(nc.semaphore(name))

        r_my = ctx.enter_context(nc.gpsimd.register("r_my"))
        r_fl = ctx.enter_context(nc.gpsimd.register("r_fl"))
        r_k = ctx.enter_context(nc.gpsimd.register("r_k"))
        r_go = ctx.enter_context(nc.gpsimd.register("r_go"))
        r_o1 = ctx.enter_context(nc.gpsimd.register("r_o1"))

        block = ctx.enter_context(nc.Block())

        def W(eng, sname, val):
            em(eng, lambda e, s=sem[sname], v=val: e.wait_ge(s, v))

        def DMA(eng, qname, out_ap, in_ap):
            v = inc(qname, 16)
            em(eng, lambda e, o=out_ap, i=in_ap, s=sem[qname]:
               e.dma_start(out=o, in_=i).then_inc(s, 16))
            return v

        def VSP():
            VN(lambda e: e.tensor_copy(out=scr16[:, 0:1], in_=epsc[:, 0:1]))

        def CSP():
            return CI(lambda e: e.activation(out=scr16[:, 1:2], in_=epsc[:, 0:1],
                                             func=AF.Copy))

        def TI(fn):
            v = inc("tt", 1)
            em("tensor", lambda e, f=fn, s=sem["tt"]: f(e).then_inc(s, 1))
            return v

        def TN(fn):
            em("tensor", lambda e, f=fn: f(e))

        def VI(fn):
            v = inc("sv", 1)
            em("vector", lambda e, f=fn, s=sem["sv"]: f(e).then_inc(s, 1))
            return v

        def VN(fn):
            em("vector", lambda e, f=fn: f(e))

        def CI(fn):
            v = inc("ss", 1)
            em("scalar", lambda e, f=fn, s=sem["ss"]: f(e).then_inc(s, 1))
            return v

        xo_r = x_own.rearrange("(a p) f -> p a f", p=128)
        xl_r = x_lite.rearrange("(a p) f -> p a f", p=128)
        for T in range(NST):
            DMA("sync", "qx", xt_own[:, 4 * T:4 * T + 4, :], xo_r[:, 4 * T:4 * T + 4, :])
        DMA("scalar", "qw", ident[:, :], ident_in[:, :])
        DMA("scalar", "qw", rolet[:, :], rolei[:, :])
        wc_r = wcatT.rearrange("(c p) k -> p c k", p=128)
        DMA("scalar", "qw", wcat32[:, :, :], wc_r[:, :, :])
        DMA("scalar", "qw", gb32[:, :, :], gb8[:, :, :])
        DMA("scalar", "qw", bcat2t[:, :], bcat2[:, :])
        DMA("scalar", "qw", ld32[:, :], ldownT[:, :])
        DMA("scalar", "qw", lu32[:, :], lupT[:, :])
        w_all = c["qw"]

        W("vector", "qw", w_all)
        VN(lambda e: e.memset(ones16[:, :], 1.0))
        VN(lambda e: e.memset(epsc[:, 0:1], EPS))
        VN(lambda e: e.memset(epsc[:, 1:2], EPS * EPS))
        VN(lambda e: e.memset(haug[32:33, :], 1.0))
        VN(lambda e: e.memset(mysum[:, 4:5], 1.0))
        VN(lambda e: e.memset(l2fm[0:1, 0, 0:K64], 1.0))
        VN(lambda e: e.memset(mysum[:, 5:6], 0.0))
        VN(lambda e: e.tensor_copy(out=wcat16[:, :, :], in_=wcat32[:, :, :]))
        VN(lambda e: e.tensor_copy(out=gb16[:, :, :], in_=gb32[:, :, :]))
        VN(lambda e: e.tensor_copy(out=ld16[:, :], in_=ld32[:, :]))
        v_casts = VI(lambda e: e.tensor_copy(out=lu16[:, :], in_=lu32[:, :]))
        d1 = DMA("scalar", "qw", stage32[0:33, :], woutaug[:, :])
        W("vector", "qw", d1)
        v1s = VI(lambda e: e.tensor_copy(out=woa16[:, :], in_=stage32[0:33, :]))
        W("scalar", "sv", v1s)
        d2 = DMA("scalar", "qw", stage32[64:65, :], bf[:, :])
        W("gpsimd", "qw", d2)
        g_bf = inc("gg", 1)
        em("gpsimd", lambda e, s=sem["qg"]: e.dma_start(
            out=corrw16[1:2, :], in_=stage32[64:65, :]).then_inc(s, 16))
        gq_bf = c["qg"] = c["qg"] + 16
        em("gpsimd", lambda e, v=gq_bf: e.wait_ge(sem["qg"], v))
        em("gpsimd", lambda e, s=sem["gg"]: e.sem_inc(s, 1))
        v2s = c["sv"]
        W("scalar", "gg", g_bf)
        d3 = DMA("scalar", "qw", stage32[0:100, :], daux[:, :])
        W("vector", "qw", d3)
        v3s = VI(lambda e: e.tensor_copy(out=da16[:, :], in_=stage32[0:100, :]))
        W("scalar", "sv", v3s)
        d4 = DMA("scalar", "qw", stage32[0:50, :], upT[:, :])
        W("vector", "qw", d4)
        v_casts = VI(lambda e: e.tensor_copy(out=up16[:, :], in_=stage32[0:50, :]))


        W("tensor", "sv", v_casts)
        for ch in range(7):
            TN(lambda e, ch=ch: e.matmul(PD[0:K64, 0:2], wcat16[:, ch, :],
                                         gb16[:, ch, :], start=(ch == 0), stop=False))
        t_mv = TI(lambda e: e.matmul(PD[0:K64, 0:2], wcat16[:, 7, :],
                                     gb16[:, 7, :], start=False, stop=True))
        W("vector", "tt", t_mv)
        for ch in range(7):
            VN(lambda e, ch=ch: e.tensor_scalar(
                out=wcat16[:, ch, :], in0=wcat32[:, ch, :],
                scalar1=gb32[:, ch, 0:1], scalar2=None, op0=OP.mult))
        VN(lambda e: e.tensor_scalar(
            out=wcat16[:, 7, :], in0=wcat32[:, 7, :],
            scalar1=gb32[:, 7, 0:1], scalar2=None, op0=OP.mult))
        VN(lambda e: e.scalar_tensor_tensor(
            out=k2wa[:, :], in0=bcat2t[:, :], scalar=1.0, in1=PD[0:K64, 0:2],
            op0=OP.mult, op1=OP.add))
        v_k2 = VI(lambda e: e.tensor_copy(out=scr16[:, 6:7], in_=epsc[:, 0:1]))
        W("tensor", "sv", v_k2)
        W("tensor", "qw", w_all)
        t_k2 = TI(lambda e: e.transpose(PD[0:2, 4:68], k2wa[:, :], ident[0:K64, 0:K64]))
        W("scalar", "tt", t_k2)
        s_k2 = CI(lambda e: e.activation(out=k2w16[:, :], in_=PD[0:2, 4:68], func=AF.Copy))

        def alpha_phase(xtile, dma_for_tile, scans_out, sumcol, tag):
            uT, pT = scans_out
            pend = [None]
            evA = {}
            evac_mark = {}
            prevpre = [0]
            def emit_tail(Tst):
                s_evac = evac_mark[Tst]
                if pend[0] is not None:
                    VN(pend[0])
                    pend[0] = None
                VSP()
                g = slice(4 * Tst, 4 * Tst + 4)
                v_var = VI(lambda e, g=g: e.tensor_scalar(
                    out=corrA[:, g, 0], in0=mvA[:, g, 0], scalar1=-1.0, scalar2=None,
                    op0=OP.mult))
                W("scalar", "sv", v_var)
                CI(lambda e, g=g: e.activation(
                    out=corrA[:, g, 1], in_=mvA[:, g, 1], func=AF.Sqrt, bias=epsc[:, 0:1], scale=1.0))
                s_sd = CSP()
                W("vector", "ss", s_sd)
                VN(lambda e, g=g: e.reciprocal(out=corrA[:, g, 2], in_=corrA[:, g, 1]))
                v_inv = VI(lambda e: e.tensor_copy(out=scr16[:, 6:7], in_=epsc[:, 0:1]))
                W("tensor", "sv", v_inv)
                for j in range(4):
                    t = 4 * Tst + j
                    TN(lambda e, t=t, j=j: e.transpose(
                        PD[0:2, 128 * j:128 * j + 128], corrA[:, t, 0:2], ident[:, :]))
                    fn = lambda e, t=t, j=j: e.transpose(
                        PD[0:1, 512 + 128 * j:512 + 128 * j + 128], corrA[:, t, 2:3],
                        ident[:, :])
                    if j < 3:
                        TN(fn)
                    else:
                        t_tiny = TI(fn)
                ro = 512 * (Tst % 2)
                W("scalar", "tt", t_tiny)
                W("scalar", "sv", c["sv"])
                CI(lambda e, ro=ro: e.activation(
                    out=rows16a[:, ro:ro + 512],
                    in_=PD[0:2, 0:512], func=AF.Copy))
                CI(lambda e, ro=ro: e.activation(
                    out=rows16b[:, ro:ro + 512],
                    in_=PD[0:1, 512:1024], func=AF.Copy))
                s_rows = CSP()
                W("tensor", "ss", s_evac)
                W("tensor", "ss", s_rows)
                W("tensor", "sv", c["sv"])
                sl = slice(512 * Tst, 512 * Tst + 512)
                par = Tst % 2
                xo = 512 * (Tst % 2)
                for ch in range(8):
                    TN(lambda e, ch=ch, xo=xo: e.matmul(
                        PC[0:K64, 0:512], wcat16[:, ch, :], xfm[:, ch, xo:xo + 512],
                        start=(ch == 0), stop=False))
                t_proj = TI(lambda e, ro=ro: e.matmul(
                    PC[0:K64, 0:512], k2w16[:, :], rows16a[0:2, ro:ro + 512],
                    start=False, stop=True))
                t_rep = TI(lambda e, ro=ro: e.matmul(
                    PC[0:K64, 512:1024], l2fm[0:1, 0, 0:K64], rows16b[0:1, ro:ro + 512],
                    start=True, stop=True))
                prevpre[0] = t_rep
                W("scalar", "tt", t_rep)
                CI(lambda e, ro=ro: e.activation(
                    out=inv1rep[:, ro:ro + 512], in_=PC[0:K64, 512:1024], func=AF.Copy))
                s_irep = CSP()
                W("vector", "tt", t_proj)
                W("vector", "ss", s_irep)
                VN(lambda e, sl=sl, ro=ro: e.tensor_tensor(
                    out=pre[:, sl], in0=inv1rep[:, ro:ro + 512], in1=PC[0:K64, 0:512],
                    op=OP.mult))
                v_pre = VI(lambda e: e.tensor_copy(out=scr16[:, 6:7], in_=epsc[:, 0:1]))

            for Tst in range(NST):
                for j in range(4):
                    t = 4 * Tst + j
                    PS = PA if t % 2 == 0 else PB
                    slot = t if xtile is xt_own else (t % 2)
                    xdma_v = dma_for_tile(t)
                    W("vector", "qx", xdma_v)
                    VN(lambda e, slot=slot, t=t: e.bn_stats(
                        out=st6t[:, t % 2, 0:6], in_=xtile[:, slot, 0:512]))
                    VN(lambda e, slot=slot, t=t: e.bn_stats(
                        out=st6t[:, t % 2, 6:12], in_=xtile[:, slot, 512:1024]))
                    if pend[0] is not None:
                        VN(pend[0])
                    pend[0] = (lambda e, t=t: e.bn_aggr(
                        out=mvA[:, t, :], in_=st6t[:, t % 2, :]))
                    W("tensor", "qx", xdma_v)
                    if t >= 2:
                        W("tensor", "ss", evA[t - 2])
                    for ch in range(8):
                        bank_col = 512 * (ch // 4) + 128 * (ch % 4)
                        fn = lambda e, ch=ch, slot=slot, PS=PS, bank_col=bank_col: \
                            e.transpose(PS[:, bank_col:bank_col + 128],
                                        xtile[:, slot, 128 * ch:128 * ch + 128], ident[:, :])
                        if ch < 7:
                            TN(fn)
                        else:
                            tv = TI(fn)
                    W("scalar", "tt", tv)
                    if Tst > 0:
                        W("scalar", "tt", prevpre[0])
                    xo = 512 * (Tst % 2)
                    for hh in range(2):
                        CI(lambda e, hh=hh, j=j, PS=PS, xo=xo: e.activation(
                            out=xfm[:, 4 * hh:4 * hh + 4, xo + 128 * j:xo + 128 * j + 128],
                            in_=PS[:, 512 * hh:512 * hh + 512].rearrange(
                                "p (c q) -> p c q", q=128),
                            func=AF.Copy))
                    evA[t] = c["ss"]
                CSP()
                evac_mark[Tst] = c["ss"]
                emit_tail(Tst)
            W("scalar", "sv", c["sv"])
            CI(lambda e: e.activation(
                out=aT[:, :], in_=pre[0:CH, :], func=AF.Sigmoid))
            s_sig = CSP()
            W("sync", "sv", c["sv"])
            d_pb = DMA("sync", "qx", preB[:, :], pre[CH:K64, :])
            W("vector", "ss", s_sig)
            W("vector", "qx", d_pb)
            VN(lambda e, uT=uT: e.tensor_tensor_scan(
                out=uT[:, :], data0=aT[:, :], data1=preB[:, :],
                initial=0.0, op0=OP.mult, op1=OP.add))
            VN(lambda e, pT=pT: e.tensor_tensor_scan(
                out=pT[:, :], data0=aT[:, :], data1=aT[:, :],
                initial=1.0, op0=OP.mult, op1=OP.bypass))
            VN(lambda e, uT=uT, sc=sumcol: e.tensor_copy(
                out=mysum[:, sc + 1:sc + 2], in_=uT[:, TOK - 1:TOK]))
            VN(lambda e, pT=pT, sc=sumcol: e.tensor_copy(
                out=mysum[:, sc:sc + 1], in_=pT[:, TOK - 1:TOK]))
            VSP()
            return VI(lambda e: e.tensor_copy(out=scr16[:, 6:7], in_=epsc[:, 0:1]))

        def ddump(name, src_ap):
            if DBG:
                W("sync", "sv", c["sv"])
                W("sync", "ss", c["ss"])
                W("sync", "tt", c["tt"])
                W("sync", "gg", c["gg"])
                v = DMA("sync", "qo", dbg[name], src_ap)
                if DBG == 1:
                    for eng in ("vector", "scalar", "tensor"):
                        W(eng, "qo", v)

        ddump("dbg_k2w", k2w16[:, :])

        v_own = alpha_phase(xt_own, lambda t: 16 * (t // 4 + 1), (uo, po), 0, "own")

        print("SEQMARK ownA_done", seqno[0])
        ddump("dbg_mva", mvA[:, :, :])
        ddump("dbg_pre", pre[:, :])
        ddump("dbg_a", aT[:, :])
        ddump("dbg_u", uo[:, :])
        ddump("dbg_p", po[:, :])
        ddump("dbg_rowsa", rows16a[:, :])
        ddump("dbg_rowsb", rows16b[:, :])

        lite_q = {}

        def lite_dma(t):
            if t not in lite_q:
                if t >= 2:
                    W("sync", "tt", c["tt"])
                    W("sync", "sv", c["sv"])
                lite_q[t] = DMA("sync", "qx", xt_l[:, t % 2:t % 2 + 1, :],
                                xl_r[:, t:t + 1, :])
            return lite_q[t]

        lite_dma(0)
        lite_dma(1)
        v_lite = alpha_phase(xt_l, lite_dma, (pre[0:CH, :], preB[:, :]), 2, "lite")

        print("SEQMARK liteA_done", seqno[0])
        W("gpsimd", "sv", v_lite)
        em("gpsimd", lambda e: e.reg_load(r_my, rolet[:1, 5:6]))
        gd1 = inc("qg", 16)
        em("gpsimd", lambda e, s=sem["qg"]: e.dma_start(
            bass.AP(ssum.tensor, r_my, [[6, CH], [1, 6]]), mysum[:, :]).then_inc(s, 16))
        em("gpsimd", lambda e, v=gd1: e.wait_ge(sem["qg"], v))
        em("gpsimd", lambda e: e.reg_load(r_fl, rolet[:1, 6:7]))
        W("vector", "qw", w_all)
        VN(lambda e: e.memset(flg[:, :], FLAGV))
        v_flg = VI(lambda e: e.tensor_copy(out=scr16[:, 6:7], in_=epsc[:, 0:1]))
        W("gpsimd", "sv", v_flg)
        gd2 = inc("qg", 16)
        em("gpsimd", lambda e, s=sem["qg"]: e.dma_start(
            bass.AP(sflg.tensor, r_fl, [[1, 1], [1, 1], [1, 1]]), flg[0:1, :]).then_inc(s, 16))
        em("gpsimd", lambda e, v=gd2: e.wait_ge(sem["qg"], v))
        em("gpsimd", lambda e: e.reg_load(r_o1, rolet[:1, 7:8]))
        em("gpsimd", lambda e: e.reg_mov(r_k, gd2))
        em("gpsimd", lambda e: e.reg_mov(r_go, 1))

        def poll_body(e):
            guard = e.While(r_go)
            guard.__enter__()
            e.dma_start(flgin[0:1, :],
                        bass.AP(sflg.tensor, r_o1, [[1, 1], [1, 1], [1, 1]])).then_inc(sem["qg"], 16)
            e.reg_add(r_k, r_k, 16)
            e.wait_ge(sem["qg"], r_k)
            e.reg_load(r_fl, flgin[:1, :1])
            ig = e.If_eq(r_fl, FLAGV)
            ig.__enter__()
            e.reg_mov(r_go, 0)
            ig.__exit__(None, None, None)
            guard.__exit__(None, None, None)
        em("gpsimd", poll_body)
        for j in range(3):
            em("gpsimd", lambda e, j=j: e.reg_load(r_o1, rolet[:1, 2 + j:3 + j]))
            em("gpsimd", lambda e, j=j, s=sem["qg"]: e.dma_start(
                carryops[:, 2 * j:2 * j + 2],
                bass.AP(ssum.tensor, r_o1, [[6, CH], [1, 2]])).then_inc(s, 16))
        em("gpsimd", lambda e: e.reg_add(r_k, r_k, 48))
        em("gpsimd", lambda e: e.wait_ge(sem["qg"], r_k))
        g_done = inc("gg", 1)
        em("gpsimd", lambda e, s=sem["gg"]: e.sem_inc(s, 1))
        W("vector", "gg", g_done)
        VN(lambda e: e.tensor_copy(out=ccbuf[:, 0:1], in_=carryops[:, 1:2]))
        VSP()
        VN(lambda e: e.scalar_tensor_tensor(
            out=ccbuf[:, 1:2], in0=carryops[:, 2:3], scalar=ccbuf[:, 0:1],
            in1=carryops[:, 3:4], op0=OP.mult, op1=OP.add))
        VSP()
        VN(lambda e: e.scalar_tensor_tensor(
            out=ccbuf[:, 2:3], in0=carryops[:, 4:5], scalar=ccbuf[:, 1:2],
            in1=carryops[:, 5:6], op0=OP.mult, op1=OP.add))
        VSP()
        VN(lambda e: e.scalar_tensor_tensor(
            out=haug[0:CH, :], in0=po[:, :], scalar=ccbuf[:, 2:3], in1=uo[:, :],
            op0=OP.mult, op1=OP.add))
        VSP()
        v_h = VI(lambda e: e.tensor_copy(out=scr16[:, 3:4], in_=epsc[:, 0:1]))

        print("SEQMARK exchange_done", seqno[0])
        ddump("dbg_mysum", mysum[:, :])
        ddump("dbg_carryops", carryops[:, :])
        ddump("dbg_cc", ccbuf[:, :])
        ddump("dbg_haug", haug[:, :])

        TN(lambda e: e.matmul(PA[0:4, 0:512], ld16[:, :], da16[:, 0:512],
                              start=True, stop=True))
        t_r1 = TI(lambda e: e.matmul(PA[0:4, 512:1024], ld16[:, :], da16[:, 512:1024],
                                     start=True, stop=True))
        W("scalar", "tt", t_r1)
        CI(lambda e: e.activation(out=r1_16[:, :], in_=PA[0:4, 0:1024], func=AF.Copy))
        s_r1 = CSP()

        W("tensor", "ss", s_r1)
        TN(lambda e: e.matmul(PB[0:50, 0:512], lu16[:, :], r1_16[:, 0:512],
                              start=True, stop=True))
        t_r2 = TI(lambda e: e.matmul(PB[0:50, 512:1024], lu16[:, :], r1_16[:, 512:1024],
                                     start=True, stop=True))
        W("scalar", "tt", t_r2)
        CI(lambda e: e.activation(out=r2_16[:, :], in_=PB[0:50, 0:1024], func=AF.Copy))
        s_r2 = CSP()
        wft_r = wfT.rearrange("(c p) o -> p c o", p=128)
        for ci in range(8):
            if ci == 0:
                W("scalar", "sv", v_casts)
            dv = DMA("scalar", "qw", stage32[:, :], wft_r[:, ci, :])
            W("tensor", "ss", s_r2)
            W("tensor", "sv", c["sv"])
            TN(lambda e, ci=ci: e.matmul(
                PC[:, 0:512], r2_16[:, 128 * ci:128 * ci + 128],
                up16[:, 0:512], start=True, stop=True))
            tv = TI(lambda e, ci=ci: e.matmul(
                PC[:, 512:1024], r2_16[:, 128 * ci:128 * ci + 128],
                up16[:, 512:1024], start=True, stop=True))
            W("vector", "qw", dv)
            W("vector", "tt", tv)
            vv = VI(lambda e, ci=ci: e.scalar_tensor_tensor(
                out=wtot16[:, ci, :], in0=stage32[:, :], scalar=1.0, in1=PC[:, :],
                op0=OP.mult, op1=OP.add))
            W("scalar", "sv", vv)
        VSP()
        v_wtot = VI(lambda e: e.tensor_copy(out=scr16[:, 6:7], in_=epsc[:, 0:1]))
        W("tensor", "sv", v_wtot)
        for h in range(2):
            for ci in range(8):
                fn = lambda e, ci=ci, h=h: e.matmul(
                    PD[0:1, 512 * h:512 * h + 512], ones16[:, :],
                    wtot16[:, ci, 512 * h:512 * h + 512], start=(ci == 0), stop=(ci == 7))
                if h == 1 and ci == 7:
                    t_s2 = TI(fn)
                else:
                    TN(fn)
        W("vector", "tt", t_s2)
        VN(lambda e: e.tensor_scalar(out=corrw16[0:1, :], in0=PD[0:1, 0:1024],
                                     scalar1=-1.0, scalar2=-1.0,
                                     op0=OP.mult, op1=OP.add))
        VSP()
        v_cw = VI(lambda e: e.tensor_copy(out=scr16[:, 5:6], in_=epsc[:, 0:1]))

        print("SEQMARK P2_done", seqno[0])


        g_weights = c["gg"]
        ddump("dbg_wtot", wtot16[:, :, :])
        ddump("dbg_corrw", corrw16[:, :])
        store_v = {}
        outp_v = {}
        v2_v = {}
        v4_v = {}
        l2ev = {}
        sqv = {}
        sclv = {}
        t_wo = {}
        v_ag2 = {}
        pend_aggr3 = [None]
        for half in range(2):
            tiles = range(8 * half, 8 * half + 8)
            if half == 0:
                W("tensor", "sv", v_h)
                W("tensor", "sv", v_cw)
                W("tensor", "gg", g_weights)

            def emit_wout(t):
                if t - 1 in v2_v:
                    W("tensor", "sv", v2_v[t - 1])
                for hh in range(2):
                    fn = lambda e, t=t, hh=hh: e.matmul(
                        PD[:, 512 * hh:512 * hh + 512], haug[:, 128 * t:128 * t + 128],
                        woa16[:, 512 * hh:512 * hh + 512], start=True, stop=True)
                    if hh == 0:
                        TN(fn)
                    else:
                        t_wo[t] = TI(fn)

            def emit_v2agg(t):
                if half == 1 and (t % 8) in outp_v:
                    W("vector", "ss", outp_v[t % 8])
                W("vector", "tt", t_wo[t])
                VN(lambda e, t=t: e.scalar_tensor_tensor(
                    out=buf8[:, t % 8, :], in0=xt_own[:, t, :], scalar=2.0,
                    in1=PD[:, :], op0=OP.mult, op1=OP.add))
                v2_v[t] = VI(lambda e: e.tensor_copy(out=scr16[:, 4:5], in_=epsc[:, 0:1]))
                VN(lambda e, t=t: e.bn_stats(out=st6t[:, 0, 0:6],
                                             in_=buf8[:, t % 8, 0:512]))
                VN(lambda e, t=t: e.bn_stats(out=st6t[:, 0, 6:12],
                                             in_=buf8[:, t % 8, 512:1024]))
                VSP()
                VN(lambda e, t=t: e.bn_aggr(out=mvB2[:, t, :], in_=st6t[:, 0, :]))
                VSP()
                v_ag2[t] = VI(lambda e: e.tensor_copy(out=scr16[:, 6:7],
                                                      in_=epsc[:, 0:1]))

            def emit_sqrt(t):
                W("scalar", "sv", v_ag2[t])
                CI(lambda e, t=t: e.activation(
                    out=mvB2[:, t, 1:2], in_=mvB2[:, t, 1:2], func=AF.Sqrt,
                    bias=epsc[:, 0:1], scale=1.0))
                sqv[t] = CSP()

            last = 8 * half + 7
            emit_wout(8 * half)
            emit_v2agg(8 * half)
            emit_sqrt(8 * half)
            for t in tiles:
                P_l = PA if t % 2 == 0 else PB
                if t - 2 in l2ev:
                    W("tensor", "ss", l2ev[t - 2])
                W("tensor", "sv", v2_v[t])
                for ch in range(8):
                    bank_col = 512 * (ch // 4) + 128 * (ch % 4)
                    fn = lambda e, ch=ch, t=t, bank_col=bank_col, P_l=P_l: e.transpose(
                        P_l[:, bank_col:bank_col + 128],
                        buf8[:, t % 8, 128 * ch:128 * ch + 128], ident[:, :])
                    if ch < 7:
                        TN(fn)
                    else:
                        t_l2 = TI(fn)
                W("tensor", "ss", sqv[t])
                if t - 1 in v4_v:
                    W("tensor", "sv", v4_v[t - 1])
                t_tb = TI(lambda e, t=t: e.transpose(
                    PC[0:2, 0:128], mvB2[:, t, :], ident[:, :]))
                if t < last:
                    emit_wout(t + 1)
                    emit_v2agg(t + 1)
                W("scalar", "tt", t_l2)
                for hh in range(2):
                    CI(lambda e, hh=hh, t=t, P_l=P_l: e.activation(
                        out=l2fm[:, t % 2, 512 * hh:512 * hh + 512],
                        in_=P_l[:, 512 * hh:512 * hh + 512],
                        func=AF.Copy))
                l2ev[t] = c["ss"]
                W("scalar", "tt", t_tb)
                CI(lambda e, t=t: e.activation(
                    out=corrL16[:, 128 * (t % 2):128 * (t % 2) + 128],
                    in_=PC[0:2, 0:128], func=AF.Copy))
                sclv[t] = CSP()
                if t < last:
                    emit_sqrt(t + 1)
                W("tensor", "ss", l2ev[t])
                for hh in range(2):
                    for ci in range(8):
                        TN(lambda e, ci=ci, hh=hh, t=t: e.matmul(
                            PC[:, 512 * hh:512 * hh + 512],
                            l2fm[:, t % 2, 128 * ci:128 * ci + 128],
                            wtot16[:, ci, 512 * hh:512 * hh + 512],
                            start=(ci == 0), stop=False))
                    if hh == 0:
                        W("tensor", "ss", sclv[t])
                    fn = lambda e, hh=hh, t=t: e.matmul(
                        PC[:, 512 * hh:512 * hh + 512],
                        corrL16[:, 128 * (t % 2):128 * (t % 2) + 128],
                        corrw16[:, 512 * hh:512 * hh + 512], start=False, stop=True)
                    if hh == 0:
                        TN(fn)
                    else:
                        t_ffn = TI(fn)
                W("vector", "tt", t_ffn)
                v4_v[t] = VI(lambda e, t=t: e.tensor_tensor(
                    out=buf8[:, t % 8, :], in0=buf8[:, t % 8, :], in1=PC[:, :],
                    op=OP.add))
                if t % 8 > 0:
                    tp = t - 1
                    sl3 = 1 + (tp % 2)
                    VN(lambda e, tp=tp, sl3=sl3: e.bn_stats(out=st6t[:, sl3, 0:6],
                                                            in_=buf8[:, tp % 8, 0:512]))
                    VN(lambda e, tp=tp, sl3=sl3: e.bn_stats(out=st6t[:, sl3, 6:12],
                                                            in_=buf8[:, tp % 8, 512:1024]))
                    if pend_aggr3[0] is not None:
                        VN(pend_aggr3[0])
                    pend_aggr3[0] = (lambda e, tp=tp, sl3=sl3: e.bn_aggr(
                        out=mvS2[:, tp, :], in_=st6t[:, sl3, :]))
            if half == 0:
                ddump("dbg_ln2", buf8[:, :, :])
            tl = 8 * half + 7
            sl3t = 1 + (tl % 2)
            VSP()
            VN(lambda e, tl=tl, sl3t=sl3t: e.bn_stats(out=st6t[:, sl3t, 0:6],
                                                      in_=buf8[:, tl % 8, 0:512]))
            VN(lambda e, tl=tl, sl3t=sl3t: e.bn_stats(out=st6t[:, sl3t, 6:12],
                                                      in_=buf8[:, tl % 8, 512:1024]))
            if pend_aggr3[0] is not None:
                VN(pend_aggr3[0])
                pend_aggr3[0] = None
            VSP()
            VN(lambda e, tl=tl, sl3t=sl3t: e.bn_aggr(out=mvS2[:, tl, :], in_=st6t[:, sl3t, :]))
            VSP()
            g = slice(8 * half, 8 * half + 8)
            W("vector", "ss", sqv[8 * half + 7])
            VN(lambda e, g=g: e.tensor_tensor(
                out=scr16b[:, g], in0=mvB2[:, g, 1], in1=mvB2[:, g, 1], op=OP.mult))
            VSP()
            VN(lambda e, g=g: e.scalar_tensor_tensor(
                out=scr16b[:, g], in0=scr16b[:, g], scalar=EPS, in1=mvS2[:, g, 1],
                op0=OP.mult, op1=OP.add))
            v_v3 = VI(lambda e: e.tensor_copy(out=scr16[:, 6:7], in_=epsc[:, 0:1]))
            W("scalar", "sv", v_v3)
            CI(lambda e, g=g: e.activation(
                out=nm3i[:, g], in_=scr16b[:, g], func=AF.Sqrt, bias=0.0, scale=1.0))
            s_sd3 = CSP()
            W("vector", "ss", s_sd3)
            VN(lambda e, g=g: e.reciprocal(out=inv3c[:, g], in_=nm3i[:, g]))
            VSP()
            VN(lambda e, g=g: e.tensor_tensor(
                out=scr16b[:, g], in0=mvS2[:, g, 0], in1=inv3c[:, g], op=OP.mult))
            VSP()
            VN(lambda e, g=g: e.tensor_scalar(
                out=nm3i[:, g], in0=scr16b[:, g], scalar1=-1.0, scalar2=None, op0=OP.mult))
            VSP()
            v_nm = VI(lambda e: e.tensor_copy(out=scr16[:, 2:3], in_=epsc[:, 0:1]))
            if half == 0:
                ddump("dbg_s", buf8[:, :, :])
                ddump("dbg_mvb", mvB2[:, :, :])
                ddump("dbg_mvs", mvS2[:, :, :])

            if half == 0:
                ddump("dbg_inv3", inv3c[:, :])
                ddump("dbg_nm3i", nm3i[:, :])
            W("scalar", "sv", v_nm)
            for t in tiles:
                if t >= 2:
                    W("scalar", "qo", store_v[t // 2 - 1])
                outp_v[t % 8] = CI(lambda e, t=t: e.activation(
                    out=outbuf[:, t % 2, :], in_=buf8[:, t % 8, :], func=AF.Identity,
                    scale=inv3c[:, t:t + 1], bias=nm3i[:, t:t + 1]))
                if t % 2 == 1:
                    CSP()
                    W("sync", "ss", c["ss"])
                    store_v[t // 2] = DMA(
                        "sync", "qo",
                        y.rearrange("(a p) f -> p a f", p=128)[:, t - 1:t + 1, :],
                        outbuf[:, :, :])
        W("sync", "qo", c["qo"])

        import os
        cut = int(os.environ.get("KMK_CUT", "0")) or None

        def run_list(eng_obj, lst):
            for s, f in lst:
                if cut is not None and s >= cut:
                    break
                f(eng_obj)

        @block.sync
        def _(sync):
            run_list(sync, ops["sync"])

        @block.scalar
        def _(scalar):
            run_list(scalar, ops["scalar"])

        @block.vector
        def _(vector):
            run_list(vector, ops["vector"])

        @block.tensor
        def _(tensor):
            run_list(tensor, ops["tensor"])

        @block.gpsimd
        def _(gpsimd):
            run_list(gpsimd, ops["gpsimd"])

    if int(os.environ.get("KMK_PRINT_SEQ", "0")):
        print("total seq:", seqno[0])
        with open("/root/problem/seqmap.txt", "w") as fh:
            for row in seqlog:
                fh.write(repr(row) + "\n")
    return nc


def kernel(**inputs):
    x = np.asarray(inputs["L0_attn_x"], dtype=np.float32)
    W_a = np.asarray(inputs["L0_attn_W_a"], dtype=np.float32)
    b_a = np.asarray(inputs["L0_attn_b_a"], dtype=np.float32)
    W_in = np.asarray(inputs["L0_attn_W_in"], dtype=np.float32)
    b_in = np.asarray(inputs["L0_attn_b_in"], dtype=np.float32)
    W_out = np.asarray(inputs["L0_attn_W_out"], dtype=np.float32)
    b_out = np.asarray(inputs["L0_attn_b_out"], dtype=np.float32)
    gamma = np.asarray(inputs["L0_attn_ln_gamma"], dtype=np.float32)
    beta = np.asarray(inputs["L0_attn_ln_beta"], dtype=np.float32)
    Wf = np.asarray(inputs["L0_ffn_W"], dtype=np.float32)
    bfv = np.asarray(inputs["L0_ffn_b"], dtype=np.float32)
    lora_down = np.asarray(inputs["L0_ffn_lora_down"], dtype=np.float32)
    lora_up = np.asarray(inputs["L0_ffn_lora_up"], dtype=np.float32)
    down_aux = np.asarray(inputs["L0_ffn_down_aux"], dtype=np.float32)
    up_aux = np.asarray(inputs["L0_ffn_up_aux"], dtype=np.float32)

    nc = build_nc()

    wcatT = np.ascontiguousarray(np.concatenate([W_a, W_in], axis=0).T)
    gb8 = np.ascontiguousarray(
        np.stack([gamma.reshape(8, 128).T, beta.reshape(8, 128).T], axis=2))
    bcat2 = np.ascontiguousarray(
        np.stack([np.zeros(64, np.float32), np.concatenate([b_a, b_in])], axis=1))
    woutaug = np.ascontiguousarray(np.concatenate([W_out.T, b_out[None, :]], axis=0))
    wfT = np.ascontiguousarray(Wf.T)
    common = {
        "wcatT": wcatT, "gb8": gb8, "bcat2": bcat2, "woutaug": woutaug,
        "wfT": wfT, "bfv": np.ascontiguousarray(bfv[None, :]),
        "ldownT": np.ascontiguousarray(lora_down.T),
        "daux": np.ascontiguousarray(down_aux),
        "lupT": np.ascontiguousarray(lora_up.T),
        "upT": np.ascontiguousarray(up_aux.T),
        "ident": np.eye(128, dtype=np.float32),
    }

    in_maps = []
    for core in range(8):
        b = core // 4
        cidx = core % 4
        own = x[b, 2048 * cidx:2048 * (cidx + 1), :]
        if cidx >= 2:
            lite = x[b, 2048 * (cidx - 2):2048 * (cidx - 1), :]
        else:
            lite = own
        par = core & 1
        myrow, otrow = par, 1 - par
        ido = myrow * 192 + 4
        if cidx == 0:
            o1, o2, o3 = ido, ido, ido
        elif cidx == 1:
            o1, o2, o3 = otrow * 192 + 0, ido, ido
        elif cidx == 2:
            o1, o2, o3 = myrow * 192 + 2, otrow * 192 + 2, ido
        else:
            o1, o2, o3 = otrow * 192 + 2, myrow * 192 + 2, otrow * 192 + 0
        rolei = np.array([[par, 1 - par, o1, o2, o3, myrow * 192, myrow, otrow]],
                         dtype=np.int32)
        m = dict(common)
        m["x_own"] = np.ascontiguousarray(own)
        m["x_lite"] = np.ascontiguousarray(lite)
        m["rolei"] = rolei
        in_maps.append(m)

    res = run_bass_kernel_spmd(nc, in_maps, core_ids=list(range(8)))
    out = np.empty((2, 8192, 1024), dtype=np.float32)
    for core in range(8):
        b = core // 4
        cidx = core % 4
        out[b, 2048 * cidx:2048 * (cidx + 1), :] = res.results[core]["y"]
    return out


if __name__ == "__main__":
    build_nc()
    print("build ok")


# version 12
# speedup vs baseline: 10.4580x; 2.9739x over previous
import numpy as np
from contextlib import ExitStack
import concourse.bass as bass
import concourse.mybir as mybir
from concourse.bass_utils import run_bass_kernel_spmd

F32 = mybir.dt.float32
F16 = mybir.dt.float16
I32 = mybir.dt.int32
AF = mybir.ActivationFunctionType
OP = mybir.AluOpType
AX = mybir.AxisListType

B, S, D = 2, 8192, 1024
CH = 32
K64 = 64
TOK = 2048
NT = 16
NST = 4
EPS = 1e-5
FLAGV = 7


def build_nc():
    import os
    DBG = int(os.environ.get("KMK_DBG", "0"))
    nc = bass.Bass("TRN2", target_bir_lowering=False)

    x_own = nc.dram_tensor("x_own", [TOK, D], F32, kind="ExternalInput").ap()
    x_lite = nc.dram_tensor("x_lite", [TOK, D], F32, kind="ExternalInput").ap()
    wcatT = nc.dram_tensor("wcatT", [D, K64], F32, kind="ExternalInput").ap()
    gb8 = nc.dram_tensor("gb8", [128, 8, 2], F32, kind="ExternalInput").ap()
    bcat2 = nc.dram_tensor("bcat2", [K64, 2], F32, kind="ExternalInput").ap()
    woutaug = nc.dram_tensor("woutaug", [33, D], F32, kind="ExternalInput").ap()
    wfT = nc.dram_tensor("wfT", [D, D], F32, kind="ExternalInput").ap()
    bf = nc.dram_tensor("bfv", [1, D], F32, kind="ExternalInput").ap()
    ldownT = nc.dram_tensor("ldownT", [100, 4], F32, kind="ExternalInput").ap()
    daux = nc.dram_tensor("daux", [100, D], F32, kind="ExternalInput").ap()
    lupT = nc.dram_tensor("lupT", [4, 50], F32, kind="ExternalInput").ap()
    upT = nc.dram_tensor("upT", [50, D], F32, kind="ExternalInput").ap()
    ident_in = nc.dram_tensor("ident", [128, 128], F32, kind="ExternalInput").ap()
    rolei = nc.dram_tensor("rolei", [1, 8], I32, kind="ExternalInput").ap()
    y = nc.dram_tensor("y", [TOK, D], F32, kind="ExternalOutput").ap()

    dbg = {}
    if DBG:
        for nm, shape, dt in [
            ("dbg_mva", [128, NT, 2], F32), ("dbg_pre", [K64, TOK], F16),
            ("dbg_a", [CH, TOK], F16), ("dbg_u", [CH, TOK], F32),
            ("dbg_p", [CH, TOK], F16), ("dbg_cc", [CH, 4], F32),
            ("dbg_haug", [33, TOK], F16), ("dbg_ln2", [128, 8, D], F32),
            ("dbg_s", [128, 8, D], F32),
            ("dbg_wtot", [128, 8, D], F16), ("dbg_k2w", [2, K64], F16),
            ("dbg_rowsa", [2, 1024], F16), ("dbg_rowsb", [1, 1024], F16),
            ("dbg_corrw", [2, D], F16),
            ("dbg_mvb", [128, NT, 2], F32), ("dbg_mvs", [128, NT, 2], F32),
            ("dbg_mysum", [CH, 6], F32), ("dbg_carryops", [CH, 6], F32),
            ("dbg_inv3", [128, NT], F32), ("dbg_nm3i", [128, NT], F32),
        ]:
            dbg[nm] = nc.dram_tensor(nm, shape, dt, kind="ExternalOutput").ap()

    ssum = nc.dram_tensor("ssum", [2, 192], F32, kind="Internal",
                          addr_space="Shared").ap()
    sflg = nc.dram_tensor("sflg", [2, 1], I32, kind="Internal",
                          addr_space="Shared").ap()

    ops = {"sync": [], "scalar": [], "vector": [], "tensor": [], "gpsimd": []}
    seqno = [0]
    import sys as _sys
    seqlog = []

    def em(eng, fn):
        fr = _sys._getframe(1)
        ln = fr.f_lineno
        fr2 = fr.f_back
        ln2 = fr2.f_lineno if fr2 else 0
        seqlog.append((seqno[0], eng, ln, ln2))
        ops[eng].append((seqno[0], fn))
        seqno[0] += 1

    c = {"qx": 0, "qw": 0, "qo": 0, "qg": 0, "sv": 0, "ss": 0, "tt": 0, "gg": 0}

    def inc(name, amt):
        c[name] += amt
        return c[name]

    with ExitStack() as ctx:
        sb = lambda name, shape, dt: ctx.enter_context(nc.sbuf_tensor(name, shape, dt))
        xt_own = sb("xt_own", [128, NT, D], F32)
        xt_l = sb("xt_l", [128, 2, D], F32)
        xfm = sb("xfm", [128, 8, 1024], F16)
        ident = sb("identt", [128, 128], F32)
        wcat32 = sb("wcat32", [128, 8, K64], F32)
        wcat16 = sb("wcat16", [128, 8, K64], F16)
        gb32 = sb("gb32", [128, 8, 2], F32)
        gb16 = sb("gb16", [128, 8, 2], F16)
        bcat2t = sb("bcat2t", [K64, 2], F32)
        k2wa = sb("k2wa", [K64, 2], F32)
        k2w16 = sb("k2w16", [2, K64], F16)
        woa16 = sb("woa16", [33, D], F16)
        stage32 = sb("stage32", [128, D], F32)
        wtot16 = sb("wtot16", [128, 8, D], F16)
        corrw16 = sb("corrw16", [2, D], F16)
        ld16 = sb("ld16", [100, 4], F16)
        da16 = sb("da16", [100, D], F16)
        lu16 = sb("lu16", [4, 50], F16)
        up16 = sb("up16", [50, D], F16)
        ld32 = sb("ld32", [100, 4], F32)
        lu32 = sb("lu32", [4, 50], F32)
        r1_16 = sb("r1_16", [4, D], F16)
        r2_16 = sb("r2_16", [50, D], F16)
        ones16 = sb("ones16", [128, 1], F16)
        st6t = sb("st6t", [128, 3, 12], F32)
        mvA = sb("mvA", [128, NT, 2], F32)
        corrA = sb("corrA", [128, NT, 3], F32)
        rows16a = sb("rows16a", [2, 1024], F16)
        rows16b = sb("rows16b", [1, 1024], F16)
        inv1rep = sb("inv1rep", [K64, 1024], F16)
        pre = sb("pre", [K64, TOK], F16)
        preB = sb("preB", [CH, TOK], F16)
        aT = sb("aT", [CH, TOK], F16)
        uo = sb("uo", [CH, TOK], F32)
        po = sb("po", [CH, TOK], F16)
        haug = sb("haug", [33, TOK], F16)
        mysum = sb("mysum", [CH, 6], F32)
        carryops = sb("carryops", [CH, 6], F32)
        ccbuf = sb("ccbuf", [CH, 4], F32)
        rolet = sb("rolet", [1, 8], I32)
        flg = sb("flg", [1, 1], I32)
        flgin = sb("flgin", [1, 1], I32)
        buf8 = sb("buf8", [128, 8, D], F32)
        l2fm = sb("l2fm", [128, 2, 1024], F16)
        mvB2 = sb("mvB2", [128, NT, 2], F32)
        mvB2b = sb("mvB2b", [128, NT, 2], F32)
        mvS2 = sb("mvS2", [128, NT, 2], F32)
        corrL16 = sb("corrL16", [2, 256], F16)
        inv3c = sb("inv3c", [128, NT], F32)
        nm3i = sb("nm3i", [128, NT], F32)
        scr16 = sb("scr16", [128, NT], F32)
        scr16b = sb("scr16b", [128, NT], F32)
        outbuf = sb("outbuf", [128, 2, D], F32)
        epsc = sb("epsc", [128, 2], F32)

        PA = ctx.enter_context(nc.psum_tensor("PA", [128, 1024], F32))
        PB = ctx.enter_context(nc.psum_tensor("PB", [128, 1024], F32))
        PC = ctx.enter_context(nc.psum_tensor("PC", [128, 1024], F32))
        PD = ctx.enter_context(nc.psum_tensor("PD", [128, 1024], F32))

        sem = {}
        for name in ("qx", "qw", "qo", "qg", "sv", "ss", "tt", "gg"):
            sem[name] = ctx.enter_context(nc.semaphore(name))

        r_my = ctx.enter_context(nc.gpsimd.register("r_my"))
        r_fl = ctx.enter_context(nc.gpsimd.register("r_fl"))
        r_k = ctx.enter_context(nc.gpsimd.register("r_k"))
        r_go = ctx.enter_context(nc.gpsimd.register("r_go"))
        r_o1 = ctx.enter_context(nc.gpsimd.register("r_o1"))

        block = ctx.enter_context(nc.Block())

        def W(eng, sname, val):
            em(eng, lambda e, s=sem[sname], v=val: e.wait_ge(s, v))

        def DMA(eng, qname, out_ap, in_ap):
            v = inc(qname, 16)
            em(eng, lambda e, o=out_ap, i=in_ap, s=sem[qname]:
               e.dma_start(out=o, in_=i).then_inc(s, 16))
            return v

        def VSP():
            VN(lambda e: e.tensor_copy(out=scr16[:, 0:1], in_=epsc[:, 0:1]))

        def CSP():
            return CI(lambda e: e.activation(out=scr16[:, 1:2], in_=epsc[:, 0:1],
                                             func=AF.Copy))

        def TI(fn):
            v = inc("tt", 1)
            em("tensor", lambda e, f=fn, s=sem["tt"]: f(e).then_inc(s, 1))
            return v

        def TN(fn):
            em("tensor", lambda e, f=fn: f(e))

        def VI(fn):
            v = inc("sv", 1)
            em("vector", lambda e, f=fn, s=sem["sv"]: f(e).then_inc(s, 1))
            return v

        def VN(fn):
            em("vector", lambda e, f=fn: f(e))

        def CI(fn):
            v = inc("ss", 1)
            em("scalar", lambda e, f=fn, s=sem["ss"]: f(e).then_inc(s, 1))
            return v

        xo_r = x_own.rearrange("(a p) f -> p a f", p=128)
        xl_r = x_lite.rearrange("(a p) f -> p a f", p=128)
        for T in range(NST):
            DMA("sync", "qx", xt_own[:, 4 * T:4 * T + 4, :], xo_r[:, 4 * T:4 * T + 4, :])
        DMA("scalar", "qw", ident[:, :], ident_in[:, :])
        DMA("scalar", "qw", rolet[:, :], rolei[:, :])
        wc_r = wcatT.rearrange("(c p) k -> p c k", p=128)
        DMA("scalar", "qw", wcat32[:, :, :], wc_r[:, :, :])
        DMA("scalar", "qw", gb32[:, :, :], gb8[:, :, :])
        DMA("scalar", "qw", bcat2t[:, :], bcat2[:, :])
        DMA("scalar", "qw", ld32[:, :], ldownT[:, :])
        DMA("scalar", "qw", lu32[:, :], lupT[:, :])
        w_all = c["qw"]

        W("vector", "qw", w_all)
        VN(lambda e: e.memset(ones16[:, :], 1.0))
        VN(lambda e: e.memset(epsc[:, 0:1], EPS))
        VN(lambda e: e.memset(epsc[:, 1:2], EPS * EPS))
        VN(lambda e: e.memset(haug[32:33, :], 1.0))
        VN(lambda e: e.memset(mysum[:, 4:5], 1.0))
        VN(lambda e: e.memset(l2fm[0:1, 0, 0:K64], 1.0))
        VN(lambda e: e.memset(mysum[:, 5:6], 0.0))
        VN(lambda e: e.tensor_copy(out=wcat16[:, :, :], in_=wcat32[:, :, :]))
        VN(lambda e: e.tensor_copy(out=gb16[:, :, :], in_=gb32[:, :, :]))
        VN(lambda e: e.tensor_copy(out=ld16[:, :], in_=ld32[:, :]))
        v_casts = VI(lambda e: e.tensor_copy(out=lu16[:, :], in_=lu32[:, :]))
        d1 = DMA("scalar", "qw", stage32[0:33, :], woutaug[:, :])
        W("vector", "qw", d1)
        v1s = VI(lambda e: e.tensor_copy(out=woa16[:, :], in_=stage32[0:33, :]))
        W("scalar", "sv", v1s)
        d2 = DMA("scalar", "qw", stage32[64:65, :], bf[:, :])
        W("gpsimd", "qw", d2)
        g_bf = inc("gg", 1)
        em("gpsimd", lambda e, s=sem["qg"]: e.dma_start(
            out=corrw16[1:2, :], in_=stage32[64:65, :]).then_inc(s, 16))
        gq_bf = c["qg"] = c["qg"] + 16
        em("gpsimd", lambda e, v=gq_bf: e.wait_ge(sem["qg"], v))
        em("gpsimd", lambda e, s=sem["gg"]: e.sem_inc(s, 1))
        v2s = c["sv"]
        W("scalar", "gg", g_bf)
        d3 = DMA("scalar", "qw", stage32[0:100, :], daux[:, :])
        W("vector", "qw", d3)
        v3s = VI(lambda e: e.tensor_copy(out=da16[:, :], in_=stage32[0:100, :]))
        W("scalar", "sv", v3s)
        d4 = DMA("scalar", "qw", stage32[0:50, :], upT[:, :])
        W("vector", "qw", d4)
        v_casts = VI(lambda e: e.tensor_copy(out=up16[:, :], in_=stage32[0:50, :]))


        W("tensor", "sv", v_casts)
        for ch in range(7):
            TN(lambda e, ch=ch: e.matmul(PD[0:K64, 0:2], wcat16[:, ch, :],
                                         gb16[:, ch, :], start=(ch == 0), stop=False))
        t_mv = TI(lambda e: e.matmul(PD[0:K64, 0:2], wcat16[:, 7, :],
                                     gb16[:, 7, :], start=False, stop=True))
        W("vector", "tt", t_mv)
        for ch in range(7):
            VN(lambda e, ch=ch: e.tensor_scalar(
                out=wcat16[:, ch, :], in0=wcat32[:, ch, :],
                scalar1=gb32[:, ch, 0:1], scalar2=None, op0=OP.mult))
        VN(lambda e: e.tensor_scalar(
            out=wcat16[:, 7, :], in0=wcat32[:, 7, :],
            scalar1=gb32[:, 7, 0:1], scalar2=None, op0=OP.mult))
        VN(lambda e: e.scalar_tensor_tensor(
            out=k2wa[:, :], in0=bcat2t[:, :], scalar=1.0, in1=PD[0:K64, 0:2],
            op0=OP.mult, op1=OP.add))
        v_k2 = VI(lambda e: e.tensor_copy(out=scr16[:, 6:7], in_=epsc[:, 0:1]))
        W("tensor", "sv", v_k2)
        W("tensor", "qw", w_all)
        t_k2 = TI(lambda e: e.transpose(PD[0:2, 4:68], k2wa[:, :], ident[0:K64, 0:K64]))
        W("scalar", "tt", t_k2)
        s_k2 = CI(lambda e: e.activation(out=k2w16[:, :], in_=PD[0:2, 4:68], func=AF.Copy))

        def alpha_phase(xtile, dma_for_tile, scans_out, sumcol, tag):
            uT, pT = scans_out
            pend = [None]
            evA = {}
            evac_mark = {}
            prevpre = [0]
            def emit_tail(Tst):
                s_evac = evac_mark[Tst]
                if pend[0] is not None:
                    VN(pend[0])
                    pend[0] = None
                VSP()
                g = slice(4 * Tst, 4 * Tst + 4)
                v_var = VI(lambda e, g=g: e.tensor_scalar(
                    out=corrA[:, g, 0], in0=mvA[:, g, 0], scalar1=-1.0, scalar2=None,
                    op0=OP.mult))
                W("scalar", "sv", v_var)
                CI(lambda e, g=g: e.activation(
                    out=corrA[:, g, 1], in_=mvA[:, g, 1], func=AF.Sqrt, bias=epsc[:, 0:1], scale=1.0))
                s_sd = CSP()
                W("vector", "ss", s_sd)
                VN(lambda e, g=g: e.reciprocal(out=corrA[:, g, 2], in_=corrA[:, g, 1]))
                v_inv = VI(lambda e: e.tensor_copy(out=scr16[:, 6:7], in_=epsc[:, 0:1]))
                W("tensor", "sv", v_inv)
                for j in range(4):
                    t = 4 * Tst + j
                    TN(lambda e, t=t, j=j: e.transpose(
                        PD[0:2, 128 * j:128 * j + 128], corrA[:, t, 0:2], ident[:, :]))
                    fn = lambda e, t=t, j=j: e.transpose(
                        PD[0:1, 512 + 128 * j:512 + 128 * j + 128], corrA[:, t, 2:3],
                        ident[:, :])
                    if j < 3:
                        TN(fn)
                    else:
                        t_tiny = TI(fn)
                ro = 512 * (Tst % 2)
                W("scalar", "tt", t_tiny)
                W("scalar", "sv", c["sv"])
                CI(lambda e, ro=ro: e.activation(
                    out=rows16a[:, ro:ro + 512],
                    in_=PD[0:2, 0:512], func=AF.Copy))
                CI(lambda e, ro=ro: e.activation(
                    out=rows16b[:, ro:ro + 512],
                    in_=PD[0:1, 512:1024], func=AF.Copy))
                s_rows = CSP()
                W("tensor", "ss", s_evac)
                W("tensor", "sv", c["sv"])
                sl = slice(512 * Tst, 512 * Tst + 512)
                par = Tst % 2
                xo = 512 * (Tst % 2)
                for ch in range(8):
                    TN(lambda e, ch=ch, xo=xo: e.matmul(
                        PC[0:K64, 0:512], wcat16[:, ch, :], xfm[:, ch, xo:xo + 512],
                        start=(ch == 0), stop=False))
                W("tensor", "ss", s_rows)
                t_proj = TI(lambda e, ro=ro: e.matmul(
                    PC[0:K64, 0:512], k2w16[:, :], rows16a[0:2, ro:ro + 512],
                    start=False, stop=True))
                t_rep = TI(lambda e, ro=ro: e.matmul(
                    PC[0:K64, 512:1024], l2fm[0:1, 0, 0:K64], rows16b[0:1, ro:ro + 512],
                    start=True, stop=True))
                prevpre[0] = t_rep
                W("scalar", "tt", t_rep)
                CI(lambda e, ro=ro: e.activation(
                    out=inv1rep[:, ro:ro + 512], in_=PC[0:K64, 512:1024], func=AF.Copy))
                s_irep = CSP()
                W("vector", "tt", t_proj)
                W("vector", "ss", s_irep)
                VN(lambda e, sl=sl, ro=ro: e.tensor_tensor(
                    out=pre[:, sl], in0=inv1rep[:, ro:ro + 512], in1=PC[0:K64, 0:512],
                    op=OP.mult))
                v_pre = VI(lambda e: e.tensor_copy(out=scr16[:, 6:7], in_=epsc[:, 0:1]))

            for Tst in range(NST):
                for j in range(4):
                    t = 4 * Tst + j
                    PS = PA if t % 2 == 0 else PB
                    slot = t if xtile is xt_own else (t % 2)
                    xdma_v = dma_for_tile(t)
                    W("vector", "qx", xdma_v)
                    VN(lambda e, slot=slot, t=t: e.bn_stats(
                        out=st6t[:, t % 2, 0:6], in_=xtile[:, slot, 0:512]))
                    VN(lambda e, slot=slot, t=t: e.bn_stats(
                        out=st6t[:, t % 2, 6:12], in_=xtile[:, slot, 512:1024]))
                    if pend[0] is not None:
                        VN(pend[0])
                    pend[0] = (lambda e, t=t: e.bn_aggr(
                        out=mvA[:, t, :], in_=st6t[:, t % 2, :]))
                    W("tensor", "qx", xdma_v)
                    if t >= 2:
                        W("tensor", "ss", evA[t - 2])
                    for ch in range(8):
                        bank_col = 512 * (ch // 4) + 128 * (ch % 4)
                        fn = lambda e, ch=ch, slot=slot, PS=PS, bank_col=bank_col: \
                            e.transpose(PS[:, bank_col:bank_col + 128],
                                        xtile[:, slot, 128 * ch:128 * ch + 128], ident[:, :])
                        if ch < 7:
                            TN(fn)
                        else:
                            tv = TI(fn)
                    W("scalar", "tt", tv)
                    if Tst > 0:
                        W("scalar", "tt", prevpre[0])
                    xo = 512 * (Tst % 2)
                    for hh in range(2):
                        CI(lambda e, hh=hh, j=j, PS=PS, xo=xo: e.activation(
                            out=xfm[:, 4 * hh:4 * hh + 4, xo + 128 * j:xo + 128 * j + 128],
                            in_=PS[:, 512 * hh:512 * hh + 512].rearrange(
                                "p (c q) -> p c q", q=128),
                            func=AF.Copy))
                    evA[t] = c["ss"]
                CSP()
                evac_mark[Tst] = c["ss"]
                emit_tail(Tst)
            W("scalar", "sv", c["sv"])
            CI(lambda e: e.activation(
                out=aT[:, :], in_=pre[0:CH, :], func=AF.Sigmoid))
            s_sig = CSP()
            W("sync", "sv", c["sv"])
            d_pb = DMA("sync", "qx", preB[:, :], pre[CH:K64, :])
            W("vector", "ss", s_sig)
            W("vector", "qx", d_pb)
            VN(lambda e, uT=uT: e.tensor_tensor_scan(
                out=uT[:, :], data0=aT[:, :], data1=preB[:, :],
                initial=0.0, op0=OP.mult, op1=OP.add))
            VN(lambda e, pT=pT: e.tensor_tensor_scan(
                out=pT[:, :], data0=aT[:, :], data1=aT[:, :],
                initial=1.0, op0=OP.mult, op1=OP.bypass))
            VN(lambda e, uT=uT, sc=sumcol: e.tensor_copy(
                out=mysum[:, sc + 1:sc + 2], in_=uT[:, TOK - 1:TOK]))
            VN(lambda e, pT=pT, sc=sumcol: e.tensor_copy(
                out=mysum[:, sc:sc + 1], in_=pT[:, TOK - 1:TOK]))
            VSP()
            return VI(lambda e: e.tensor_copy(out=scr16[:, 6:7], in_=epsc[:, 0:1]))

        def ddump(name, src_ap):
            if DBG:
                W("sync", "sv", c["sv"])
                W("sync", "ss", c["ss"])
                W("sync", "tt", c["tt"])
                W("sync", "gg", c["gg"])
                v = DMA("sync", "qo", dbg[name], src_ap)
                if DBG == 1:
                    for eng in ("vector", "scalar", "tensor"):
                        W(eng, "qo", v)

        ddump("dbg_k2w", k2w16[:, :])

        v_own = alpha_phase(xt_own, lambda t: 16 * (t // 4 + 1), (uo, po), 0, "own")

        print("SEQMARK ownA_done", seqno[0])
        ddump("dbg_mva", mvA[:, :, :])
        ddump("dbg_pre", pre[:, :])
        ddump("dbg_a", aT[:, :])
        ddump("dbg_u", uo[:, :])
        ddump("dbg_p", po[:, :])
        ddump("dbg_rowsa", rows16a[:, :])
        ddump("dbg_rowsb", rows16b[:, :])

        lite_q = {}

        def lite_dma(t):
            if t not in lite_q:
                if t >= 2:
                    W("sync", "tt", c["tt"])
                    W("sync", "sv", c["sv"])
                lite_q[t] = DMA("sync", "qx", xt_l[:, t % 2:t % 2 + 1, :],
                                xl_r[:, t:t + 1, :])
            return lite_q[t]

        lite_dma(0)
        lite_dma(1)
        v_lite = alpha_phase(xt_l, lite_dma, (pre[0:CH, :], preB[:, :]), 2, "lite")

        print("SEQMARK liteA_done", seqno[0])
        W("gpsimd", "sv", v_lite)
        em("gpsimd", lambda e: e.reg_load(r_my, rolet[:1, 5:6]))
        gd1 = inc("qg", 16)
        em("gpsimd", lambda e, s=sem["qg"]: e.dma_start(
            bass.AP(ssum.tensor, r_my, [[6, CH], [1, 6]]), mysum[:, :]).then_inc(s, 16))
        em("gpsimd", lambda e, v=gd1: e.wait_ge(sem["qg"], v))
        em("gpsimd", lambda e: e.reg_load(r_fl, rolet[:1, 6:7]))
        W("vector", "qw", w_all)
        VN(lambda e: e.memset(flg[:, :], FLAGV))
        v_flg = VI(lambda e: e.tensor_copy(out=scr16[:, 6:7], in_=epsc[:, 0:1]))
        W("gpsimd", "sv", v_flg)
        gd2 = inc("qg", 16)
        em("gpsimd", lambda e, s=sem["qg"]: e.dma_start(
            bass.AP(sflg.tensor, r_fl, [[1, 1], [1, 1], [1, 1]]), flg[0:1, :]).then_inc(s, 16))
        em("gpsimd", lambda e, v=gd2: e.wait_ge(sem["qg"], v))
        em("gpsimd", lambda e: e.reg_load(r_o1, rolet[:1, 7:8]))
        em("gpsimd", lambda e: e.reg_mov(r_k, gd2))
        em("gpsimd", lambda e: e.reg_mov(r_go, 1))

        def poll_body(e):
            guard = e.While(r_go)
            guard.__enter__()
            e.dma_start(flgin[0:1, :],
                        bass.AP(sflg.tensor, r_o1, [[1, 1], [1, 1], [1, 1]])).then_inc(sem["qg"], 16)
            e.reg_add(r_k, r_k, 16)
            e.wait_ge(sem["qg"], r_k)
            e.reg_load(r_fl, flgin[:1, :1])
            ig = e.If_eq(r_fl, FLAGV)
            ig.__enter__()
            e.reg_mov(r_go, 0)
            ig.__exit__(None, None, None)
            guard.__exit__(None, None, None)
        em("gpsimd", poll_body)
        for j in range(3):
            em("gpsimd", lambda e, j=j: e.reg_load(r_o1, rolet[:1, 2 + j:3 + j]))
            em("gpsimd", lambda e, j=j, s=sem["qg"]: e.dma_start(
                carryops[:, 2 * j:2 * j + 2],
                bass.AP(ssum.tensor, r_o1, [[6, CH], [1, 2]])).then_inc(s, 16))
        em("gpsimd", lambda e: e.reg_add(r_k, r_k, 48))
        em("gpsimd", lambda e: e.wait_ge(sem["qg"], r_k))
        g_done = inc("gg", 1)
        em("gpsimd", lambda e, s=sem["gg"]: e.sem_inc(s, 1))
        W("vector", "gg", g_done)
        VN(lambda e: e.tensor_copy(out=ccbuf[:, 0:1], in_=carryops[:, 1:2]))
        VSP()
        VN(lambda e: e.scalar_tensor_tensor(
            out=ccbuf[:, 1:2], in0=carryops[:, 2:3], scalar=ccbuf[:, 0:1],
            in1=carryops[:, 3:4], op0=OP.mult, op1=OP.add))
        VSP()
        VN(lambda e: e.scalar_tensor_tensor(
            out=ccbuf[:, 2:3], in0=carryops[:, 4:5], scalar=ccbuf[:, 1:2],
            in1=carryops[:, 5:6], op0=OP.mult, op1=OP.add))
        VSP()
        VN(lambda e: e.scalar_tensor_tensor(
            out=haug[0:CH, :], in0=po[:, :], scalar=ccbuf[:, 2:3], in1=uo[:, :],
            op0=OP.mult, op1=OP.add))
        VSP()
        v_h = VI(lambda e: e.tensor_copy(out=scr16[:, 3:4], in_=epsc[:, 0:1]))

        print("SEQMARK exchange_done", seqno[0])
        ddump("dbg_mysum", mysum[:, :])
        ddump("dbg_carryops", carryops[:, :])
        ddump("dbg_cc", ccbuf[:, :])
        ddump("dbg_haug", haug[:, :])

        TN(lambda e: e.matmul(PA[0:4, 0:512], ld16[:, :], da16[:, 0:512],
                              start=True, stop=True))
        t_r1 = TI(lambda e: e.matmul(PA[0:4, 512:1024], ld16[:, :], da16[:, 512:1024],
                                     start=True, stop=True))
        W("scalar", "tt", t_r1)
        CI(lambda e: e.activation(out=r1_16[:, :], in_=PA[0:4, 0:1024], func=AF.Copy))
        s_r1 = CSP()

        W("tensor", "ss", s_r1)
        TN(lambda e: e.matmul(PB[0:50, 0:512], lu16[:, :], r1_16[:, 0:512],
                              start=True, stop=True))
        t_r2 = TI(lambda e: e.matmul(PB[0:50, 512:1024], lu16[:, :], r1_16[:, 512:1024],
                                     start=True, stop=True))
        W("scalar", "tt", t_r2)
        CI(lambda e: e.activation(out=r2_16[:, :], in_=PB[0:50, 0:1024], func=AF.Copy))
        s_r2 = CSP()
        wft_r = wfT.rearrange("(c p) o -> p c o", p=128)
        for ci in range(8):
            if ci == 0:
                W("scalar", "sv", v_casts)
            dv = DMA("scalar", "qw", stage32[:, :], wft_r[:, ci, :])
            W("tensor", "ss", s_r2)
            W("tensor", "sv", c["sv"])
            TN(lambda e, ci=ci: e.matmul(
                PC[:, 0:512], r2_16[:, 128 * ci:128 * ci + 128],
                up16[:, 0:512], start=True, stop=True))
            tv = TI(lambda e, ci=ci: e.matmul(
                PC[:, 512:1024], r2_16[:, 128 * ci:128 * ci + 128],
                up16[:, 512:1024], start=True, stop=True))
            W("vector", "qw", dv)
            W("vector", "tt", tv)
            vv = VI(lambda e, ci=ci: e.scalar_tensor_tensor(
                out=wtot16[:, ci, :], in0=stage32[:, :], scalar=1.0, in1=PC[:, :],
                op0=OP.mult, op1=OP.add))
            W("scalar", "sv", vv)
        VSP()
        v_wtot = VI(lambda e: e.tensor_copy(out=scr16[:, 6:7], in_=epsc[:, 0:1]))
        W("tensor", "sv", v_wtot)
        for h in range(2):
            for ci in range(8):
                fn = lambda e, ci=ci, h=h: e.matmul(
                    PD[0:1, 512 * h:512 * h + 512], ones16[:, :],
                    wtot16[:, ci, 512 * h:512 * h + 512], start=(ci == 0), stop=(ci == 7))
                if h == 1 and ci == 7:
                    t_s2 = TI(fn)
                else:
                    TN(fn)
        W("vector", "tt", t_s2)
        VN(lambda e: e.tensor_scalar(out=corrw16[0:1, :], in0=PD[0:1, 0:1024],
                                     scalar1=-1.0, scalar2=-1.0,
                                     op0=OP.mult, op1=OP.add))
        VSP()
        v_cw = VI(lambda e: e.tensor_copy(out=scr16[:, 5:6], in_=epsc[:, 0:1]))

        print("SEQMARK P2_done", seqno[0])


        g_weights = c["gg"]
        ddump("dbg_wtot", wtot16[:, :, :])
        ddump("dbg_corrw", corrw16[:, :])
        store_v = {}
        outp_v = {}
        v2_v = {}
        v4_v = {}
        l2ev = {}
        sqv = {}
        sclv = {}
        t_wo = {}
        v_ag2 = {}
        pend_aggr3 = [None]
        for half in range(2):
            tiles = range(8 * half, 8 * half + 8)
            if half == 0:
                W("tensor", "sv", v_h)
                W("tensor", "sv", v_cw)
                W("tensor", "gg", g_weights)

            def emit_wout(t):
                if t - 1 in v2_v:
                    W("tensor", "sv", v2_v[t - 1])
                for hh in range(2):
                    fn = lambda e, t=t, hh=hh: e.matmul(
                        PD[:, 512 * hh:512 * hh + 512], haug[:, 128 * t:128 * t + 128],
                        woa16[:, 512 * hh:512 * hh + 512], start=True, stop=True)
                    if hh == 0:
                        TN(fn)
                    else:
                        t_wo[t] = TI(fn)

            def emit_v2agg(t):
                if half == 1 and (t % 8) in outp_v:
                    W("vector", "ss", outp_v[t % 8])
                W("vector", "tt", t_wo[t])
                VN(lambda e, t=t: e.scalar_tensor_tensor(
                    out=buf8[:, t % 8, :], in0=xt_own[:, t, :], scalar=2.0,
                    in1=PD[:, :], op0=OP.mult, op1=OP.add))
                v2_v[t] = VI(lambda e: e.tensor_copy(out=scr16[:, 4:5], in_=epsc[:, 0:1]))
                VN(lambda e, t=t: e.bn_stats(out=st6t[:, 0, 0:6],
                                             in_=buf8[:, t % 8, 0:512]))
                VN(lambda e, t=t: e.bn_stats(out=st6t[:, 0, 6:12],
                                             in_=buf8[:, t % 8, 512:1024]))
                VSP()
                VN(lambda e, t=t: e.bn_aggr(out=mvB2[:, t, :], in_=st6t[:, 0, :]))
                VSP()
                v_ag2[t] = VI(lambda e: e.tensor_copy(out=scr16[:, 6:7],
                                                      in_=epsc[:, 0:1]))

            def emit_sqrt(t):
                W("scalar", "sv", v_ag2[t])
                CI(lambda e, t=t: e.activation(
                    out=mvB2[:, t, 1:2], in_=mvB2[:, t, 1:2], func=AF.Sqrt,
                    bias=epsc[:, 0:1], scale=1.0))
                sqv[t] = CSP()

            last = 8 * half + 7
            emit_wout(8 * half)
            emit_v2agg(8 * half)
            emit_sqrt(8 * half)
            for t in tiles:
                P_l = PA if t % 2 == 0 else PB
                if t - 2 in l2ev:
                    W("tensor", "ss", l2ev[t - 2])
                W("tensor", "sv", v2_v[t])
                for ch in range(8):
                    bank_col = 512 * (ch // 4) + 128 * (ch % 4)
                    fn = lambda e, ch=ch, t=t, bank_col=bank_col, P_l=P_l: e.transpose(
                        P_l[:, bank_col:bank_col + 128],
                        buf8[:, t % 8, 128 * ch:128 * ch + 128], ident[:, :])
                    if ch < 7:
                        TN(fn)
                    else:
                        t_l2 = TI(fn)
                W("tensor", "ss", sqv[t])
                if t - 1 in v4_v:
                    W("tensor", "sv", v4_v[t - 1])
                t_tb = TI(lambda e, t=t: e.transpose(
                    PC[0:2, 0:128], mvB2[:, t, :], ident[:, :]))
                if t < last:
                    emit_wout(t + 1)
                    emit_v2agg(t + 1)
                W("scalar", "tt", t_l2)
                for hh in range(2):
                    CI(lambda e, hh=hh, t=t, P_l=P_l: e.activation(
                        out=l2fm[:, t % 2, 512 * hh:512 * hh + 512],
                        in_=P_l[:, 512 * hh:512 * hh + 512],
                        func=AF.Copy))
                l2ev[t] = c["ss"]
                W("scalar", "tt", t_tb)
                CI(lambda e, t=t: e.activation(
                    out=corrL16[:, 128 * (t % 2):128 * (t % 2) + 128],
                    in_=PC[0:2, 0:128], func=AF.Copy))
                sclv[t] = CSP()
                if t < last:
                    emit_sqrt(t + 1)
                W("tensor", "ss", l2ev[t])
                for hh in range(2):
                    for ci in range(8):
                        TN(lambda e, ci=ci, hh=hh, t=t: e.matmul(
                            PC[:, 512 * hh:512 * hh + 512],
                            l2fm[:, t % 2, 128 * ci:128 * ci + 128],
                            wtot16[:, ci, 512 * hh:512 * hh + 512],
                            start=(ci == 0), stop=False))
                    if hh == 0:
                        W("tensor", "ss", sclv[t])
                    fn = lambda e, hh=hh, t=t: e.matmul(
                        PC[:, 512 * hh:512 * hh + 512],
                        corrL16[:, 128 * (t % 2):128 * (t % 2) + 128],
                        corrw16[:, 512 * hh:512 * hh + 512], start=False, stop=True)
                    if hh == 0:
                        TN(fn)
                    else:
                        t_ffn = TI(fn)
                W("vector", "tt", t_ffn)
                v4_v[t] = VI(lambda e, t=t: e.tensor_tensor(
                    out=buf8[:, t % 8, :], in0=buf8[:, t % 8, :], in1=PC[:, :],
                    op=OP.add))
                if t % 8 > 0:
                    tp = t - 1
                    sl3 = 1 + (tp % 2)
                    VN(lambda e, tp=tp, sl3=sl3: e.bn_stats(out=st6t[:, sl3, 0:6],
                                                            in_=buf8[:, tp % 8, 0:512]))
                    VN(lambda e, tp=tp, sl3=sl3: e.bn_stats(out=st6t[:, sl3, 6:12],
                                                            in_=buf8[:, tp % 8, 512:1024]))
                    if pend_aggr3[0] is not None:
                        VN(pend_aggr3[0])
                    pend_aggr3[0] = (lambda e, tp=tp, sl3=sl3: e.bn_aggr(
                        out=mvS2[:, tp, :], in_=st6t[:, sl3, :]))
            if half == 0:
                ddump("dbg_ln2", buf8[:, :, :])
            tl = 8 * half + 7
            sl3t = 1 + (tl % 2)
            VSP()
            VN(lambda e, tl=tl, sl3t=sl3t: e.bn_stats(out=st6t[:, sl3t, 0:6],
                                                      in_=buf8[:, tl % 8, 0:512]))
            VN(lambda e, tl=tl, sl3t=sl3t: e.bn_stats(out=st6t[:, sl3t, 6:12],
                                                      in_=buf8[:, tl % 8, 512:1024]))
            if pend_aggr3[0] is not None:
                VN(pend_aggr3[0])
                pend_aggr3[0] = None
            VSP()
            VN(lambda e, tl=tl, sl3t=sl3t: e.bn_aggr(out=mvS2[:, tl, :], in_=st6t[:, sl3t, :]))
            VSP()
            g = slice(8 * half, 8 * half + 8)
            W("vector", "ss", sqv[8 * half + 7])
            VN(lambda e, g=g: e.tensor_tensor(
                out=scr16b[:, g], in0=mvB2[:, g, 1], in1=mvB2[:, g, 1], op=OP.mult))
            VSP()
            VN(lambda e, g=g: e.scalar_tensor_tensor(
                out=scr16b[:, g], in0=scr16b[:, g], scalar=EPS, in1=mvS2[:, g, 1],
                op0=OP.mult, op1=OP.add))
            v_v3 = VI(lambda e: e.tensor_copy(out=scr16[:, 6:7], in_=epsc[:, 0:1]))
            W("scalar", "sv", v_v3)
            CI(lambda e, g=g: e.activation(
                out=nm3i[:, g], in_=scr16b[:, g], func=AF.Sqrt, bias=0.0, scale=1.0))
            s_sd3 = CSP()
            W("vector", "ss", s_sd3)
            VN(lambda e, g=g: e.reciprocal(out=inv3c[:, g], in_=nm3i[:, g]))
            VSP()
            VN(lambda e, g=g: e.tensor_tensor(
                out=scr16b[:, g], in0=mvS2[:, g, 0], in1=inv3c[:, g], op=OP.mult))
            VSP()
            VN(lambda e, g=g: e.tensor_scalar(
                out=nm3i[:, g], in0=scr16b[:, g], scalar1=-1.0, scalar2=None, op0=OP.mult))
            VSP()
            v_nm = VI(lambda e: e.tensor_copy(out=scr16[:, 2:3], in_=epsc[:, 0:1]))
            if half == 0:
                ddump("dbg_s", buf8[:, :, :])
                ddump("dbg_mvb", mvB2[:, :, :])
                ddump("dbg_mvs", mvS2[:, :, :])

            if half == 0:
                ddump("dbg_inv3", inv3c[:, :])
                ddump("dbg_nm3i", nm3i[:, :])
            W("scalar", "sv", v_nm)
            for t in tiles:
                if t >= 2:
                    W("scalar", "qo", store_v[t // 2 - 1])
                outp_v[t % 8] = CI(lambda e, t=t: e.activation(
                    out=outbuf[:, t % 2, :], in_=buf8[:, t % 8, :], func=AF.Identity,
                    scale=inv3c[:, t:t + 1], bias=nm3i[:, t:t + 1]))
                if t % 2 == 1:
                    CSP()
                    W("sync", "ss", c["ss"])
                    store_v[t // 2] = DMA(
                        "sync", "qo",
                        y.rearrange("(a p) f -> p a f", p=128)[:, t - 1:t + 1, :],
                        outbuf[:, :, :])
        W("sync", "qo", c["qo"])

        import os
        cut = int(os.environ.get("KMK_CUT", "0")) or None

        def run_list(eng_obj, lst):
            for s, f in lst:
                if cut is not None and s >= cut:
                    break
                f(eng_obj)

        @block.sync
        def _(sync):
            run_list(sync, ops["sync"])

        @block.scalar
        def _(scalar):
            run_list(scalar, ops["scalar"])

        @block.vector
        def _(vector):
            run_list(vector, ops["vector"])

        @block.tensor
        def _(tensor):
            run_list(tensor, ops["tensor"])

        @block.gpsimd
        def _(gpsimd):
            run_list(gpsimd, ops["gpsimd"])

    if int(os.environ.get("KMK_PRINT_SEQ", "0")):
        print("total seq:", seqno[0])
        with open("/root/problem/seqmap.txt", "w") as fh:
            for row in seqlog:
                fh.write(repr(row) + "\n")
    return nc


def kernel(**inputs):
    x = np.asarray(inputs["L0_attn_x"], dtype=np.float32)
    W_a = np.asarray(inputs["L0_attn_W_a"], dtype=np.float32)
    b_a = np.asarray(inputs["L0_attn_b_a"], dtype=np.float32)
    W_in = np.asarray(inputs["L0_attn_W_in"], dtype=np.float32)
    b_in = np.asarray(inputs["L0_attn_b_in"], dtype=np.float32)
    W_out = np.asarray(inputs["L0_attn_W_out"], dtype=np.float32)
    b_out = np.asarray(inputs["L0_attn_b_out"], dtype=np.float32)
    gamma = np.asarray(inputs["L0_attn_ln_gamma"], dtype=np.float32)
    beta = np.asarray(inputs["L0_attn_ln_beta"], dtype=np.float32)
    Wf = np.asarray(inputs["L0_ffn_W"], dtype=np.float32)
    bfv = np.asarray(inputs["L0_ffn_b"], dtype=np.float32)
    lora_down = np.asarray(inputs["L0_ffn_lora_down"], dtype=np.float32)
    lora_up = np.asarray(inputs["L0_ffn_lora_up"], dtype=np.float32)
    down_aux = np.asarray(inputs["L0_ffn_down_aux"], dtype=np.float32)
    up_aux = np.asarray(inputs["L0_ffn_up_aux"], dtype=np.float32)

    nc = build_nc()

    wcatT = np.ascontiguousarray(np.concatenate([W_a, W_in], axis=0).T)
    gb8 = np.ascontiguousarray(
        np.stack([gamma.reshape(8, 128).T, beta.reshape(8, 128).T], axis=2))
    bcat2 = np.ascontiguousarray(
        np.stack([np.zeros(64, np.float32), np.concatenate([b_a, b_in])], axis=1))
    woutaug = np.ascontiguousarray(np.concatenate([W_out.T, b_out[None, :]], axis=0))
    wfT = np.ascontiguousarray(Wf.T)
    common = {
        "wcatT": wcatT, "gb8": gb8, "bcat2": bcat2, "woutaug": woutaug,
        "wfT": wfT, "bfv": np.ascontiguousarray(bfv[None, :]),
        "ldownT": np.ascontiguousarray(lora_down.T),
        "daux": np.ascontiguousarray(down_aux),
        "lupT": np.ascontiguousarray(lora_up.T),
        "upT": np.ascontiguousarray(up_aux.T),
        "ident": np.eye(128, dtype=np.float32),
    }

    in_maps = []
    for core in range(8):
        b = core // 4
        cidx = core % 4
        own = x[b, 2048 * cidx:2048 * (cidx + 1), :]
        if cidx >= 2:
            lite = x[b, 2048 * (cidx - 2):2048 * (cidx - 1), :]
        else:
            lite = own
        par = core & 1
        myrow, otrow = par, 1 - par
        ido = myrow * 192 + 4
        if cidx == 0:
            o1, o2, o3 = ido, ido, ido
        elif cidx == 1:
            o1, o2, o3 = otrow * 192 + 0, ido, ido
        elif cidx == 2:
            o1, o2, o3 = myrow * 192 + 2, otrow * 192 + 2, ido
        else:
            o1, o2, o3 = otrow * 192 + 2, myrow * 192 + 2, otrow * 192 + 0
        rolei = np.array([[par, 1 - par, o1, o2, o3, myrow * 192, myrow, otrow]],
                         dtype=np.int32)
        m = dict(common)
        m["x_own"] = np.ascontiguousarray(own)
        m["x_lite"] = np.ascontiguousarray(lite)
        m["rolei"] = rolei
        in_maps.append(m)

    res = run_bass_kernel_spmd(nc, in_maps, core_ids=list(range(8)))
    out = np.empty((2, 8192, 1024), dtype=np.float32)
    for core in range(8):
        b = core // 4
        cidx = core % 4
        out[b, 2048 * cidx:2048 * (cidx + 1), :] = res.results[core]["y"]
    return out


if __name__ == "__main__":
    build_nc()
    print("build ok")
